# version 02
# baseline (speedup 1.0000x reference)
import numpy as np
import ml_dtypes
import concourse.bass as bass
import concourse.mybir as mybir
from concourse.alu_op_type import AluOpType as A
from concourse.bass_utils import run_bass_kernel_spmd
from contextlib import ExitStack

F32 = mybir.dt.float32
BF16 = mybir.dt.bfloat16
F16 = mybir.dt.float16
I32 = mybir.dt.int32
U8 = mybir.dt.uint8
AF = mybir.ActivationFunctionType
BF = ml_dtypes.bfloat16

D, S, B, NCORES, RPC = 4096, 4096, 2, 8, 1024
NH, HD, KVH, NSLOT, KEEP = 32, 128, 8, 8, 820
NDT = D // 128


def core_rows(c):
    j = c % 4
    return np.concatenate([np.arange(128 * (4 * s + j), 128 * (4 * s + j) + 128)
                           for s in range(NSLOT)])


class Ct:
    def __init__(self):
        self.c = {}

    def bump(self, inst, sem, n=1):
        inst.then_inc(sem, n)
        self.c[id(sem)] = self.c.get(id(sem), 0) + n
        return self.c[id(sem)]

    def val(self, sem):
        return self.c.get(id(sem), 0)

    def wait(self, eng, sem, v=None):
        eng.wait_ge(sem, self.val(sem) if v is None else v)


def _sb(ctx, nc, name, shape, dt):
    return ctx.enter_context(nc.sbuf_tensor(name, shape, dt))[:, :]


def _ps(ctx, nc, name, shape, dt=F32):
    return ctx.enter_context(nc.psum_tensor(name, shape, dt))[:, :]


def emit_ln(nc, ct, sm, xbf, cols, onec, oner, sq, ps_m, ps_v, ps_bc, st,
            rstd_bc, mr_bc, out, eps, rms=False, w_slices=None, ntile=NDT, stb=None):
    pe, ve, se = nc.tensor, nc.vector, nc.scalar
    s_pe, s_ve, s_se = sm["pe"], sm["ve"], sm["se"]
    xt = lambda t: xbf[:, t * cols:(t + 1) * cols]
    ot = lambda t: out[:, t * cols:(t + 1) * cols]

    nck = (cols + 511) // 512
    cchunk = lambda ap, k: ap[:, k * 512:min((k + 1) * 512, cols)]
    if not rms:
        for t in range(ntile):
            for k in range(nck):
                pe.matmul(cchunk(ps_m, k), lhsT=onec, rhs=cchunk(xt(t), k),
                          start=(t == 0), stop=(t == ntile - 1))
    pe_base = ct.val(s_pe)
    for t in range(ntile):
        if t >= 2:
            ct.wait(se, s_pe, pe_base + t - 1)
        ct.bump(se.activation(out=sq[:, (t % 2) * cols:(t % 2 + 1) * cols],
                              in_=xt(t), func=AF.Square), s_se)
        ct.wait(pe, s_se)
        for k in range(nck):
            last = pe.matmul(cchunk(ps_v, k), lhsT=onec,
                             rhs=sq[:, (t % 2) * cols + k * 512:(t % 2) * cols + min((k + 1) * 512, cols)],
                             start=(t == 0), stop=(t == ntile - 1))
        ct.bump(last, s_pe)
    m = st[:, 0 * cols:1 * cols]
    v = st[:, 1 * cols:2 * cols]
    t2 = st[:, 2 * cols:3 * cols]
    rec = st[:, 1 * cols:2 * cols]
    rstd = st[:, 3 * cols:4 * cols]
    mr = st[:, 0 * cols:1 * cols]
    ct.wait(ve, s_pe)
    ve.tensor_scalar(out=v, in0=ps_v, scalar1=1.0 / D, scalar2=None, op0=A.mult)
    if not rms:
        ve.tensor_scalar(out=m, in0=ps_m, scalar1=1.0 / D, scalar2=None, op0=A.mult)
        ve.drain()
        ve.tensor_tensor(out=t2, in0=m, in1=m, op=A.mult)
        ve.drain()
        ve.tensor_tensor(out=t2, in0=v, in1=t2, op=A.subtract)
        ve.drain()
        ve.tensor_scalar(out=t2, in0=t2, scalar1=eps, scalar2=None, op0=A.add)
    else:
        ve.drain()
        ve.tensor_scalar(out=t2, in0=v, scalar1=eps, scalar2=None, op0=A.add)
    ve.drain()
    ct.bump(ve.reciprocal(out=rec, in_=t2), s_ve)
    ct.wait(se, s_ve)
    ct.bump(se.activation(out=rstd, in_=rec, func=AF.Sqrt), s_se)
    ct.wait(ve, s_se)
    rstd_b = stb[:, 0:cols]
    mr_b = stb[:, cols:2 * cols]
    ve.tensor_copy(out=rstd_b, in_=rstd)
    if not rms:
        ve.tensor_tensor(out=mr, in0=m, in1=rstd, op=A.mult)
        ve.drain()
        ct.bump(ve.tensor_copy(out=mr_b, in_=mr), s_ve)
    else:
        ve.drain()
        ct.bump(ve.tensor_copy(out=rstd_b, in_=rstd), s_ve)
    ct.wait(pe, s_ve)
    lastmm = None
    for k in range(nck):
        lastmm = pe.matmul(ps_bc[:, k * 512:min((k + 1) * 512, cols)], lhsT=oner,
                           rhs=cchunk(rstd_b, k), start=True, stop=True)
        if not rms:
            lastmm = pe.matmul(ps_bc[:, cols + k * 512:cols + min((k + 1) * 512, cols)],
                               lhsT=oner, rhs=cchunk(mr_b, k), start=True, stop=True)
    ct.bump(lastmm, s_pe)
    ct.wait(ve, s_pe)
    ve.tensor_copy(out=rstd_bc, in_=ps_bc[:, 0:cols])
    if not rms:
        ve.tensor_copy(out=mr_bc, in_=ps_bc[:, cols:2 * cols])
    ve.drain()
    last = None
    for t in range(ntile):
        last = ve.tensor_tensor(out=ot(t), in0=xt(t), in1=rstd_bc, op=A.mult)
    if not rms:
        ve.drain()
        for t in range(ntile):
            last = ve.tensor_tensor(out=ot(t), in0=ot(t), in1=mr_bc, op=A.subtract)
    if w_slices is not None:
        ve.drain()
        for t in range(ntile):
            last = ve.tensor_scalar(out=ot(t), in0=ot(t), scalar1=w_slices[t],
                                    scalar2=None, op0=A.mult)
    ct.bump(last, s_ve)


def build_l0ffn():
    nc = bass.Bass("TRN2", target_bir_lowering=False)
    x1T = nc.dram_tensor("x1T", [D, RPC], F32, kind="ExternalInput").ap()
    w0T = nc.dram_tensor("w0T", [D, D], F32, kind="ExternalInput").ap()
    b0t = nc.dram_tensor("b0t", [128, NDT], F32, kind="ExternalInput").ap()
    aT = nc.dram_tensor("aT", [D, 64], F32, kind="ExternalInput").ap()
    bTs = nc.dram_tensor("bTs", [64, D], F32, kind="ExternalInput").ap()
    labT = nc.dram_tensor("labT", [D, RPC], U8, kind="ExternalInput").ap()
    x2T = nc.dram_tensor("x2T", [D, RPC], F32, kind="ExternalOutput").ap()

    ct = Ct()
    with ExitStack() as ctx:
        xbf = _sb(ctx, nc, "xbf", [128, NDT * RPC], BF16)
        x1f = _sb(ctx, nc, "x1f", [128, 4 * 512], F32)
        sq = _sb(ctx, nc, "sq", [128, 2 * RPC], BF16)
        st = _sb(ctx, nc, "st", [1, 4 * RPC], F32)
        stb = _sb(ctx, nc, "stb", [1, 2 * RPC], BF16)
        rstd_bc = _sb(ctx, nc, "rstd_bc", [128, RPC], BF16)
        mr_bc = _sb(ctx, nc, "mr_bc", [128, RPC], BF16)
        onec = _sb(ctx, nc, "onec", [128, 1], BF16)
        oner = _sb(ctx, nc, "oner", [1, 128], BF16)
        w0sb = _sb(ctx, nc, "w0sb", [128, 2 * NDT * 128], BF16)
        atsb = _sb(ctx, nc, "atsb", [128, NDT * 64], BF16)
        btsb = _sb(ctx, nc, "btsb", [16, 4 * D], BF16)
        b0sb = _sb(ctx, nc, "b0sb", [128, NDT], F32)
        hlsb = _sb(ctx, nc, "hlsb", [16, 4 * RPC], BF16)
        lab = _sb(ctx, nc, "lab", [128, 4 * 512], U8)
        selb = _sb(ctx, nc, "selb", [128, 4 * 512], BF16)
        accb = _sb(ctx, nc, "accb", [128, 512], BF16)
        outf = _sb(ctx, nc, "outf", [128, 4 * 512], F32)

        ps = _ps(ctx, nc, "ps", [128, 8 * 512])
        ps_m = ps[0:1, 0:RPC]
        ps_v = ps[0:1, RPC:2 * RPC]
        ps_bc = ps[:, 2 * RPC:4 * RPC]
        ps_y = ps[:, 0:2 * 512]
        ps_l = ps[:, 2 * 512:6 * 512]
        ps_h = ps[0:16, 6 * 512:7 * 512]

        s_ld = ctx.enter_context(nc.semaphore("s_ld"))
        s_pe = ctx.enter_context(nc.semaphore("s_pe"))
        s_ve = ctx.enter_context(nc.semaphore("s_ve"))
        s_se = ctx.enter_context(nc.semaphore("s_se"))
        s_st = ctx.enter_context(nc.semaphore("s_st"))
        s_w = ctx.enter_context(nc.semaphore("s_w"))
        s_rx = ctx.enter_context(nc.semaphore("s_rx"))
        s_gp = ctx.enter_context(nc.semaphore("s_gp"))
        sm = dict(pe=s_pe, ve=s_ve, se=s_se)
        blk = ctx.enter_context(nc.Block())

        pe, ve, se, gp, sy = nc.tensor, nc.vector, nc.scalar, nc.gpsimd, nc.sync

        gp.memset(onec, 1.0)
        ct.bump(gp.memset(oner, 1.0), s_gp, 1)
        ct.bump(gp.dma_start(out=xbf.rearrange("p (t c) -> p t c", t=NDT),
                             in_=x1T.rearrange("(t p) c -> p t c", p=128)), s_ld, 16)
        ct.bump(gp.dma_start(out=atsb.rearrange("p (t c) -> p t c", t=NDT),
                             in_=aT.rearrange("(t p) c -> p t c", p=128)), s_ld, 16)
        ct.bump(gp.dma_start(out=btsb.rearrange("r (l d) -> r l d", l=4),
                     in_=bTs.rearrange("(l r) d -> r l d", r=16)), s_ld, 16)
        ct.bump(gp.dma_start(out=b0sb, in_=b0t), s_ld, 16)

        def load_w0(ot, buf):
            return ct.bump(gp.dma_start(
                out=w0sb[:, buf * NDT * 128:(buf + 1) * NDT * 128]
                    .rearrange("p (t c) -> p t c", t=NDT),
                in_=w0T[:, ot * 128:(ot + 1) * 128].rearrange("(t p) c -> p t c", p=128)),
                s_w, 16)

        load_w0(0, 0)

        ct.wait(pe, s_gp)
        ct.wait(pe, s_ld)
        ct.wait(se, s_ld)
        emit_ln(nc, ct, sm, xbf, RPC, onec, oner, sq, ps_m, ps_v, ps_bc, st,
                rstd_bc, mr_bc, xbf, 1e-5, rms=False, stb=stb)
        xn_ready_ve = ct.val(s_ve)
        xn = lambda t: xbf[:, t * RPC:(t + 1) * RPC]

        ct.wait(pe, s_ld)
        ct.wait(pe, s_ve, xn_ready_ve)
        for l in range(4):
            for rb in range(2):
                for t in range(NDT):
                    i = pe.matmul(ps_h, lhsT=atsb[:, t * 64 + l * 16:t * 64 + (l + 1) * 16],
                                  rhs=xn(t)[:, rb * 512:(rb + 1) * 512],
                                  start=(t == 0), stop=(t == NDT - 1))
                ct.bump(i, s_pe)
                ct.wait(ve, s_pe)
                ct.bump(ve.tensor_copy(
                    out=hlsb[:, l * RPC + rb * 512:l * RPC + (rb + 1) * 512],
                    in_=ps_h), s_ve)
                ct.wait(pe, s_ve)

        for ot in range(NDT):
            if ot + 1 < NDT:
                pass
            wbuf = ot % 2
            wv = lambda t: w0sb[:, wbuf * NDT * 128 + t * 128:wbuf * NDT * 128 + (t + 1) * 128]
            ct.wait(pe, s_w, 16 * (ot + 1))
            ct.bump(sy.dma_start(
                out=x1f[:, (ot % 2) * 2 * 512:(ot % 2 + 1) * 2 * 512],
                in_=x1T[ot * 128:(ot + 1) * 128, :]), s_rx, 16)
            ct.bump(sy.dma_start(
                out=lab[:, (ot % 2) * 2 * 512:(ot % 2 + 1) * 2 * 512],
                in_=labT[ot * 128:(ot + 1) * 128, :]), s_rx, 16)
            for rb in range(2):
                py = ps_y[:, rb * 512:(rb + 1) * 512]
                for t in range(NDT):
                    pe.matmul(py, lhsT=wv(t), rhs=xn(t)[:, rb * 512:(rb + 1) * 512],
                              start=(t == 0), stop=(t == NDT - 1))
                for l in range(4):
                    pl = ps_l[:, l * 512:(l + 1) * 512]
                    i = pe.matmul(pl, lhsT=btsb[:, l * D + ot * 128:l * D + (ot + 1) * 128],
                                  rhs=hlsb[:, l * RPC + rb * 512:l * RPC + (rb + 1) * 512],
                                  start=True, stop=True)
                ct.bump(i, s_pe)
                ct.wait(ve, s_pe)
                ct.wait(ve, s_rx)
                lblk = lab[:, (ot % 2) * 1024 + rb * 512:(ot % 2) * 1024 + (rb + 1) * 512]
                for l in range(4):
                    ve.tensor_scalar(out=selb[:, l * 512:(l + 1) * 512], in0=lblk,
                                     scalar1=float(l), scalar2=None, op0=A.is_equal)
                ve.drain()
                for l in range(4):
                    ve.tensor_tensor(out=selb[:, l * 512:(l + 1) * 512],
                                     in0=selb[:, l * 512:(l + 1) * 512],
                                     in1=ps_l[:, l * 512:(l + 1) * 512], op=A.mult)
                ve.drain()
                ve.tensor_tensor(out=selb[:, 0:512], in0=selb[:, 0:512],
                                 in1=selb[:, 512:1024], op=A.add)
                ve.tensor_tensor(out=selb[:, 1024:1536], in0=selb[:, 1024:1536],
                                 in1=selb[:, 1536:2048], op=A.add)
                ve.drain()
                ve.tensor_tensor(out=accb, in0=selb[:, 0:512], in1=selb[:, 1024:1536], op=A.add)
                ve.drain()
                ofb = outf[:, (ot % 2) * 1024 + rb * 512:(ot % 2) * 1024 + (rb + 1) * 512]
                x1b = x1f[:, (ot % 2) * 1024 + rb * 512:(ot % 2) * 1024 + (rb + 1) * 512]
                ve.scalar_tensor_tensor(out=ofb, in0=accb, scalar=8.0, in1=py,
                                        op0=A.mult, op1=A.add)
                ve.drain()
                ve.tensor_scalar(out=ofb, in0=ofb, scalar1=b0sb[:, ot:ot + 1],
                                 scalar2=None, op0=A.add)
                ve.drain()
                ct.bump(ve.tensor_tensor(out=ofb, in0=ofb, in1=x1b, op=A.add), s_ve)
                ct.wait(pe, s_ve)
            if ot + 1 < NDT:
                ct.wait(gp, s_pe)
                load_w0(ot + 1, (ot + 1) % 2)
            ct.wait(sy, s_ve)
            ct.wait(sy, s_st)
            ct.bump(sy.dma_start(out=x2T[ot * 128:(ot + 1) * 128, :],
                                 in_=outf[:, (ot % 2) * 1024:(ot % 2) * 1024 + 1024]),
                    s_st, 16)
            if ot >= 1:
                ct.wait(ve, s_st, 16 * ot)
            ct.wait(sy, s_ve)
        ct.wait(gp, s_st)
    return nc


def _shard_xT(x_full):
    out = []
    for c in range(NCORES):
        rows = core_rows(c)
        out.append(np.ascontiguousarray(x_full[c // 4][rows, :].T))
    return out


def _gather_xT(shards):
    x = np.zeros((B, S, D), np.float32)
    for c in range(NCORES):
        rows = core_rows(c)
        x[c // 4][rows, :] = np.asarray(shards[c]).T
    return x


_L0FFN_CACHE = {}


def run_l0ffn(x1_full, W0, b0, Aw, Bw, labelT_full):
    if "nc" not in _L0FFN_CACHE:
        _L0FFN_CACHE["nc"] = build_l0ffn()
    nc = _L0FFN_CACHE["nc"]
    w0T = np.ascontiguousarray(W0.T)
    b0t = np.ascontiguousarray(b0.reshape(NDT, 128).T)
    aT = np.ascontiguousarray(Aw.reshape(64, D).T)
    bTs = np.ascontiguousarray(Bw.transpose(0, 2, 1).reshape(64, D))
    xsh = _shard_xT(x1_full)
    in_maps = []
    for c in range(NCORES):
        rows = core_rows(c)
        labT = np.ascontiguousarray(labelT_full[c // 4][:, rows])
        in_maps.append(dict(x1T=xsh[c], w0T=w0T, b0t=b0t, aT=aT, bTs=bTs, labT=labT))
    res = run_bass_kernel_spmd(nc, in_maps, core_ids=list(range(NCORES)))
    return _gather_xT([r["x2T"] for r in res.results])


def _rotl32(x, r):
    return ((x << np.uint32(r)) | (x >> np.uint32(32 - r))).astype(np.uint32)


def _threefry2x32(k1, k2, x1, x2):
    ks0, ks1 = np.uint32(k1), np.uint32(k2)
    ks2 = np.uint32(0x1BD11BDA) ^ ks0 ^ ks1
    x = [(x1 + ks0).astype(np.uint32), (x2 + ks1).astype(np.uint32)]
    rotations = [[13, 15, 26, 6], [17, 29, 16, 24]]
    ks = [ks0, ks1, ks2]
    for i in range(5):
        for r in rotations[i % 2]:
            x[0] = (x[0] + x[1]).astype(np.uint32)
            x[1] = _rotl32(x[1], r) ^ x[0]
        x[0] = (x[0] + ks[(i + 1) % 3]).astype(np.uint32)
        x[1] = (x[1] + ks[(i + 2) % 3] + np.uint32(i + 1)).astype(np.uint32)
    return x


def _np_uniform_key42(shape):
    n = int(np.prod(shape))
    idx = np.arange(n, dtype=np.uint64)
    c1 = (idx >> np.uint64(32)).astype(np.uint32)
    c2 = (idx & np.uint64(0xFFFFFFFF)).astype(np.uint32)
    r1, r2 = _threefry2x32(np.uint32(0), np.uint32(42), c1, c2)
    bits = r1 ^ r2
    f = ((bits >> np.uint32(9)) | np.uint32(0x3F800000)).view(np.float32) - np.float32(1.0)
    return np.maximum(f, np.float32(0.0)).reshape(shape)


def _dropout_labels():
    u = _np_uniform_key42((B, S, D))
    lab = np.zeros((B, S, D), np.uint8)
    for l in range(4):
        m = (u >= np.float32(l * 0.25)) & (u < np.float32((l + 1) * 0.25))
        lab[m] = l
    return np.ascontiguousarray(lab.transpose(0, 2, 1))




DQW = 2048


def _emit_dequant(nc, ct, q, K, N, Kg, qw, qz, sc, scr_sc, scr_zsc, wst, bufs):
    pe, ve, se, gp, sy = nc.tensor, nc.vector, nc.scalar, nc.gpsimd, nc.sync
    s_dql, s_dqs, s_ve = q["dql"], q["dqs"], q["ve"]
    SH = [0, 16, 4, 20, 8, 24, 12, 28]
    nch = (N + DQW - 1) // DQW
    for c in range(nch):
        w = min(DQW, N - c * DQW)
        ct.wait(gp, s_ve)
        ct.bump(gp.dma_start(out=bufs["qzb"][0:Kg, 0:w // 8],
                             in_=qz[:, c * DQW // 8:c * DQW // 8 + w // 8]), s_dql, 16)
        ct.bump(gp.dma_start(out=bufs["scf"][0:Kg, 0:w],
                             in_=sc[:, c * DQW:c * DQW + w]), s_dql, 16)
        ct.wait(ve, s_dql)
        ct.wait(ve, s_dqs)
        for p, sh in enumerate(SH):
            ve.tensor_scalar(out=bufs["zint"][0:Kg, p:w:8], in0=bufs["qzb"][0:Kg, 0:w // 8],
                             scalar1=sh, scalar2=15,
                             op0=A.logical_shift_right, op1=A.bitwise_and)
        ve.drain()
        ve.tensor_tensor(out=bufs["zscb"][0:Kg, 0:w], in0=bufs["zint"][0:Kg, 0:w],
                         in1=bufs["scf"][0:Kg, 0:w], op=A.mult)
        ct.bump(ve.tensor_copy(out=bufs["scb2"][0:Kg, 0:w], in_=bufs["scf"][0:Kg, 0:w]), s_ve)
        ct.wait(gp, s_ve)
        ct.bump(gp.dma_start(out=scr_zsc[:, c * DQW:c * DQW + w], in_=bufs["zscb"][0:Kg, 0:w]), s_dqs, 16)
        ct.bump(gp.dma_start(out=scr_sc[:, c * DQW:c * DQW + w], in_=bufs["scb2"][0:Kg, 0:w]), s_dqs, 16)
    ct.wait(gp, s_dqs)
    nkt = K // 128
    ti = 0
    vbase = ct.val(q["ve"])
    for kt in range(nkt):
        for c in range(nch):
            w = min(DQW, N - c * DQW)
            bb = ti % 2
            if ti >= 2:
                ct.wait(gp, s_ve, vbase + ti - 1)
            ct.bump(gp.dma_start(out=bufs["qwb"][bb][:, 0:w // 8],
                                 in_=qw[kt * 128:(kt + 1) * 128,
                                       c * DQW // 8:c * DQW // 8 + w // 8]), s_dql, 16)
            ct.bump(gp.dma_start(out=bufs["scbc"][bb][:, 0:w],
                                 in_=scr_sc[kt:kt + 1, c * DQW:c * DQW + w].broadcast_to([128, w])),
                    s_dql, 16)
            ct.bump(gp.dma_start(out=bufs["zscbc"][bb][:, 0:w],
                                 in_=scr_zsc[kt:kt + 1, c * DQW:c * DQW + w].broadcast_to([128, w])),
                    s_dql, 16)
            ct.wait(ve, s_dql)
            if ti >= 2:
                ct.wait(ve, s_dqs)
            for p, sh in enumerate(SH):
                ve.tensor_scalar(out=bufs["wint"][:, p:w:8], in0=bufs["qwb"][bb][:, 0:w // 8],
                                 scalar1=sh, scalar2=15,
                                 op0=A.logical_shift_right, op1=A.bitwise_and)
            ve.drain()
            ve.tensor_tensor(out=bufs["wbf"][bb][:, 0:w], in0=bufs["wint"][:, 0:w],
                             in1=bufs["scbc"][bb][:, 0:w], op=A.mult)
            ve.drain()
            ct.bump(ve.tensor_tensor(out=bufs["wbf"][bb][:, 0:w], in0=bufs["wbf"][bb][:, 0:w],
                                     in1=bufs["zscbc"][bb][:, 0:w], op=A.subtract), s_ve)
            ct.wait(gp, s_ve)
            ct.bump(gp.dma_start(
                out=wst[c * DQW // 128:c * DQW // 128 + w // 128, kt * 128:(kt + 1) * 128, :]
                    .rearrange("o p m -> p o m"),
                in_=bufs["wbf"][bb][:, 0:w].rearrange("p (o m) -> p o m", m=128)), s_dqs, 16)
            ti += 1
    ct.wait(gp, s_dqs)


def _emit_fm_gemm(nc, ct, q, wst, nkt, not_, rhs_tile, wtl, ps_banks, evac, s_w,
                  nrb=2, rbw=512, wait_before=None, dbuf=True):
    pe, ve, se, gp, sy = nc.tensor, nc.vector, nc.scalar, nc.gpsimd, nc.sync
    s_ve = q["ve"]
    w_base = ct.val(s_w)
    ct.wait(sy, q["dqs"])
    ct.wait(sy, q["pe"])

    def loadw(ot, buf):
        ct.wait(sy, s_w)
        ct.bump(sy.dma_start(
            out=wtl[:, buf * nkt * 128:(buf + 1) * nkt * 128].rearrange("p (t m) -> p t m", t=nkt),
            in_=wst[ot].rearrange("(t p) m -> p t m", p=128)), s_w, 16)

    loadw(not_[0], 0)
    for i, ot in enumerate(not_):
        bb = (i % 2) if dbuf else 0
        if dbuf and i + 1 < len(not_):
            if i >= 1:
                ct.wait(sy, q["pe"])
            loadw(not_[i + 1], (i + 1) % 2)
        ct.wait(pe, s_w, w_base + 16 * (i + 1))
        if wait_before is not None:
            wait_before(ot)
        wv = lambda t: wtl[:, bb * nkt * 128 + t * 128:bb * nkt * 128 + (t + 1) * 128]
        for rb in range(nrb):
            ct.wait(pe, s_ve, ct.val(s_ve))
            for t in range(nkt):
                ii = pe.matmul(ps_banks[rb], lhsT=wv(t), rhs=rhs_tile(t, rb),
                               start=(t == 0), stop=(t == nkt - 1))
            ct.bump(ii, q["pe"])
            evac(ot, rb, ps_banks[rb])
        if (not dbuf) and i + 1 < len(not_):
            ct.wait(sy, q["pe"])
            loadw(not_[i + 1], 0)


def build_l1attn():
    nc = bass.Bass("TRN2", target_bir_lowering=False)
    x2T = nc.dram_tensor("x2T", [D, RPC], F32, kind="ExternalInput").ap()
    ln1t = nc.dram_tensor("ln1t", [128, NDT], F32, kind="ExternalInput").ap()
    ln2t = nc.dram_tensor("ln2t", [128, NDT], F32, kind="ExternalInput").ap()
    qkv_qw = nc.dram_tensor("qkv_qw", [D, 512], I32, kind="ExternalInput").ap()
    qkv_qz = nc.dram_tensor("qkv_qz", [32, 512], I32, kind="ExternalInput").ap()
    qkv_sc = nc.dram_tensor("qkv_sc", [32, D], F32, kind="ExternalInput").ap()
    o_qw = nc.dram_tensor("o_qw", [D, 512], I32, kind="ExternalInput").ap()
    o_qz = nc.dram_tensor("o_qz", [32, 512], I32, kind="ExternalInput").ap()
    o_sc = nc.dram_tensor("o_sc", [32, D], F32, kind="ExternalInput").ap()
    gu_qw = nc.dram_tensor("gu_qw", [D, 2752], I32, kind="ExternalInput").ap()
    gu_qz = nc.dram_tensor("gu_qz", [32, 2752], I32, kind="ExternalInput").ap()
    gu_sc = nc.dram_tensor("gu_sc", [32, 22016], F32, kind="ExternalInput").ap()
    dn_qw = nc.dram_tensor("dn_qw", [11008, 512], I32, kind="ExternalInput").ap()
    dn_qz = nc.dram_tensor("dn_qz", [86, 512], I32, kind="ExternalInput").ap()
    dn_sc = nc.dram_tensor("dn_sc", [86, D], F32, kind="ExternalInput").ap()
    x3T = nc.dram_tensor("x3T", [D, RPC], F32, kind="ExternalOutput").ap()

    qkv_st = nc.dram_tensor("qkv_st", [32, D, 128], BF16, kind="Internal").ap()
    o_st = nc.dram_tensor("o_st", [32, D, 128], BF16, kind="Internal").ap()
    gu_st = nc.dram_tensor("gu_st", [172, D, 128], BF16, kind="Internal").ap()
    dn_st = nc.dram_tensor("dn_st", [32, 11008, 128], BF16, kind="Internal").ap()
    sc_qkv = nc.dram_tensor("sc_qkv", [32, D], BF16, kind="Internal").ap()
    zs_qkv = nc.dram_tensor("zs_qkv", [32, D], BF16, kind="Internal").ap()
    sc_o = nc.dram_tensor("sc_o", [32, D], BF16, kind="Internal").ap()
    zs_o = nc.dram_tensor("zs_o", [32, D], BF16, kind="Internal").ap()
    sc_gu = nc.dram_tensor("sc_gu", [32, 22016], BF16, kind="Internal").ap()
    zs_gu = nc.dram_tensor("zs_gu", [32, 22016], BF16, kind="Internal").ap()
    sc_dn = nc.dram_tensor("sc_dn", [86, D], BF16, kind="Internal").ap()
    zs_dn = nc.dram_tensor("zs_dn", [86, D], BF16, kind="Internal").ap()
    xaD = nc.dram_tensor("xaD", [D, RPC], F32, kind="Internal").ap()
    gD = nc.dram_tensor("gD", [D, RPC], F32, kind="Internal").ap()
    h3D = nc.dram_tensor("h3D", [86, 128, RPC], BF16, kind="Internal").ap()

    ct = Ct()
    with ExitStack() as ctx:
        xbf = _sb(ctx, nc, "xbf", [128, NDT * RPC], BF16)
        s_dql = ctx.enter_context(nc.semaphore("s_dql"))
        s_dqs = ctx.enter_context(nc.semaphore("s_dqs"))
        s_pe = ctx.enter_context(nc.semaphore("s_pe"))
        s_ve = ctx.enter_context(nc.semaphore("s_ve"))
        s_se = ctx.enter_context(nc.semaphore("s_se"))
        s_w = ctx.enter_context(nc.semaphore("s_w"))
        s_rx = ctx.enter_context(nc.semaphore("s_rx"))
        s_rx2 = ctx.enter_context(nc.semaphore("s_rx2"))
        s_st = ctx.enter_context(nc.semaphore("s_st"))
        s_ld = ctx.enter_context(nc.semaphore("s_ld"))
        s_h3 = ctx.enter_context(nc.semaphore("s_h3"))
        sm = dict(pe=s_pe, ve=s_ve, se=s_se)
        q = dict(dql=s_dql, dqs=s_dqs, pe=s_pe, ve=s_ve, se=s_se)
        blk = ctx.enter_context(nc.Block())
        pe, ve, se, gp, sy = nc.tensor, nc.vector, nc.scalar, nc.gpsimd, nc.sync

        with ExitStack() as dctx:
            dbufs = dict(
                qzb=_sb(dctx, nc, "qzb", [86, DQW // 8], I32),
                scf=_sb(dctx, nc, "scf", [86, DQW], F32),
                zint=_sb(dctx, nc, "zint", [86, DQW], I32),
                zscb=_sb(dctx, nc, "zscb", [86, DQW], BF16),
                scb2=_sb(dctx, nc, "scb2", [86, DQW], BF16),
                qwb=[_sb(dctx, nc, "qwb0", [128, DQW // 8], I32),
                     _sb(dctx, nc, "qwb1", [128, DQW // 8], I32)],
                wint=_sb(dctx, nc, "wint", [128, DQW], I32),
                wbf=[_sb(dctx, nc, "wbf0", [128, DQW], BF16),
                     _sb(dctx, nc, "wbf1", [128, DQW], BF16)],
                scbc=[_sb(dctx, nc, "scbc0", [128, DQW], BF16),
                      _sb(dctx, nc, "scbc1", [128, DQW], BF16)],
                zscbc=[_sb(dctx, nc, "zscbc0", [128, DQW], BF16),
                       _sb(dctx, nc, "zscbc1", [128, DQW], BF16)],
            )
            _emit_dequant(nc, ct, q, D, D, 32, qkv_qw, qkv_qz, qkv_sc, sc_qkv, zs_qkv, qkv_st, dbufs)
            _emit_dequant(nc, ct, q, D, D, 32, o_qw, o_qz, o_sc, sc_o, zs_o, o_st, dbufs)
            _emit_dequant(nc, ct, q, D, 22016, 32, gu_qw, gu_qz, gu_sc, sc_gu, zs_gu, gu_st, dbufs)
            _emit_dequant(nc, ct, q, 11008, D, 86, dn_qw, dn_qz, dn_sc, sc_dn, zs_dn, dn_st, dbufs)

        actT = _sb(ctx, nc, "actT", [128, NDT * RPC], BF16)
        wtl = _sb(ctx, nc, "wtl", [128, 86 * 128], BF16)
        sq = _sb(ctx, nc, "sq", [128, 2 * RPC], BF16)
        st = _sb(ctx, nc, "st", [1, 4 * RPC], F32)
        stb = _sb(ctx, nc, "stb", [1, 2 * RPC], BF16)
        rstd_bc = _sb(ctx, nc, "rstd_bc", [128, RPC], BF16)
        mr_bc = _sb(ctx, nc, "mr_bc", [128, RPC], BF16)
        onec = _sb(ctx, nc, "onec", [128, 1], BF16)
        oner = _sb(ctx, nc, "oner", [1, 128], BF16)
        lnw = _sb(ctx, nc, "lnw", [128, 2 * NDT], F32)
        xrf = _sb(ctx, nc, "xrf", [128, 2 * RPC], F32)
        outf = _sb(ctx, nc, "outf", [128, 2 * RPC], F32)
        sgb = _sb(ctx, nc, "sgb", [128, RPC], BF16)
        sub = _sb(ctx, nc, "sub", [128, RPC], BF16)
        h3b = _sb(ctx, nc, "h3b", [128, 2 * RPC], BF16)

        ps = _ps(ctx, nc, "ps", [128, 8 * 512])
        ps_m = ps[0:1, 4 * 512:4 * 512 + RPC]
        ps_v = ps[0:1, 6 * 512:6 * 512 + RPC]
        ps_bc = ps[:, 4 * 512:8 * 512]

        gp.memset(onec, 1.0)
        ct.bump(gp.memset(oner, 1.0), s_se, 1)
        ct.bump(gp.dma_start(out=xbf.rearrange("p (t c) -> p t c", t=NDT),
                             in_=x2T.rearrange("(t p) c -> p t c", p=128)), s_ld, 16)
        ct.bump(gp.dma_start(out=lnw[:, 0:NDT], in_=ln1t), s_ld, 16)
        ct.bump(gp.dma_start(out=lnw[:, NDT:2 * NDT], in_=ln2t), s_ld, 16)

        ct.wait(pe, s_se)
        ct.wait(pe, s_ld)
        ct.wait(se, s_ld)
        ct.wait(ve, s_ld)
        emit_ln(nc, ct, sm, xbf, RPC, onec, oner, sq, ps_m, ps_v, ps_bc, st,
                rstd_bc, mr_bc, xbf, 1e-5, rms=False, stb=stb)
        ve.drain()
        for t in range(NDT):
            if t >= 2:
                ct.wait(ve, s_st)
            ct.bump(ve.tensor_copy(out=outf[:, (t % 2) * RPC:(t % 2 + 1) * RPC],
                                   in_=xbf[:, t * RPC:(t + 1) * RPC]), s_ve)
            ct.wait(sy, s_ve)
            ct.bump(sy.dma_start(out=gD[t * 128:(t + 1) * 128, :],
                                 in_=outf[:, (t % 2) * RPC:(t % 2 + 1) * RPC]), s_st, 16)

        ct.wait(pe, s_ve)
        ct.wait(se, s_ve)
        ln1_sl = [lnw[:, t:t + 1] for t in range(NDT)]
        emit_ln(nc, ct, sm, xbf, RPC, onec, oner, sq, ps_m, ps_v, ps_bc, st,
                rstd_bc, mr_bc, xbf, 1e-6, rms=True, w_slices=ln1_sl, stb=stb)
        xn = lambda t: xbf[:, t * RPC:(t + 1) * RPC]
        ct.wait(pe, s_ve)

        def evac_qkv(ot, rb, p):
            ct.wait(se, q["pe"])
            ct.bump(se.activation(out=actT[:, ot * RPC + rb * 512:ot * RPC + (rb + 1) * 512],
                                  in_=p, func=AF.Copy), s_ve)
        _emit_fm_gemm(nc, ct, q, qkv_st, NDT, list(range(NDT)),
                      lambda t, rb: xn(t)[:, rb * 512:(rb + 1) * 512],
                      wtl, [ps[:, 0:512], ps[:, 512:1024]], evac_qkv, s_w)

        at = lambda t: actT[:, t * RPC:(t + 1) * RPC]

        gd_total = ct.val(s_st)

        def wait_x2(ot):
            if ot == 0:
                ct.wait(sy, s_st, gd_total)
            ct.wait(sy, s_rx)
            ct.bump(sy.dma_start(out=xrf[:, (ot % 2) * RPC:(ot % 2 + 1) * RPC],
                                 in_=gD[ot * 128:(ot + 1) * 128, :]), s_rx, 16)

        def evac_o(ot, rb, p):
            ct.wait(ve, q["pe"])
            ct.wait(ve, s_rx)
            ct.wait(ve, s_st)
            ofb = outf[:, (ot % 2) * RPC + rb * 512:(ot % 2) * RPC + (rb + 1) * 512]
            x2b = xrf[:, (ot % 2) * RPC + rb * 512:(ot % 2) * RPC + (rb + 1) * 512]
            ve.tensor_tensor(out=ofb, in0=p, in1=x2b, op=A.add)
            ve.drain()
            ct.bump(ve.tensor_copy(out=xn(ot)[:, rb * 512:(rb + 1) * 512], in_=ofb), s_ve)
            if rb == 1:
                ct.wait(sy, s_ve)
                ct.wait(sy, s_st)
                ct.bump(sy.dma_start(out=xaD[ot * 128:(ot + 1) * 128, :],
                                     in_=outf[:, (ot % 2) * RPC:(ot % 2 + 1) * RPC]), s_st, 16)
        _emit_fm_gemm(nc, ct, q, o_st, NDT, list(range(NDT)),
                      lambda t, rb: at(t)[:, rb * 512:(rb + 1) * 512],
                      wtl, [ps[:, 0:512], ps[:, 512:1024]], evac_o, s_w,
                      wait_before=wait_x2)

        ct.wait(pe, s_ve)
        ct.wait(se, s_ve)
        ln2_sl = [lnw[:, NDT + t:NDT + t + 1] for t in range(NDT)]
        emit_ln(nc, ct, sm, xbf, RPC, onec, oner, sq, ps_m, ps_v, ps_bc, st,
                rstd_bc, mr_bc, xbf, 1e-6, rms=True, w_slices=ln2_sl, stb=stb)
        ct.wait(pe, s_ve)

        def gu_pair(i):
            for half, ot in ((0, i), (1, 86 + i)):
                bb = (2 * i + half) % 2
                ct.wait(sy, s_dqs)
                ct.wait(sy, s_w)
                ct.wait(sy, s_pe)
                ct.bump(sy.dma_start(
                    out=wtl[:, bb * NDT * 128:(bb + 1) * NDT * 128].rearrange("p (t m) -> p t m", t=NDT),
                    in_=gu_st[ot].rearrange("(t p) m -> p t m", p=128)), s_w, 16)
                ct.wait(pe, s_w)
                for rb in range(2):
                    pbank = ps[:, (2 * half + rb) * 512:(2 * half + rb + 1) * 512]
                    ct.wait(pe, s_ve, ct.val(s_ve))
                    ct.wait(pe, s_se)
                    for t in range(NDT):
                        ii = pe.matmul(pbank, lhsT=wtl[:, bb * NDT * 128 + t * 128:bb * NDT * 128 + (t + 1) * 128],
                                       rhs=xn(t)[:, rb * 512:(rb + 1) * 512],
                                       start=(t == 0), stop=(t == NDT - 1))
                    ct.bump(ii, s_pe)
            ct.wait(se, s_pe)
            ct.wait(se, s_ve)
            se.activation(out=sgb[:, 0:512], in_=ps[:, 0:512], func=AF.Silu)
            ct.bump(se.activation(out=sgb[:, 512:1024], in_=ps[:, 512:1024], func=AF.Silu), s_se)
            ct.wait(ve, s_pe)
            ve.tensor_copy(out=sub[:, 0:512], in_=ps[:, 1024:1536])
            ve.tensor_copy(out=sub[:, 512:1024], in_=ps[:, 1536:2048])
            ct.wait(ve, s_se)
            ct.wait(ve, s_h3)
            ve.drain()
            ct.bump(ve.tensor_tensor(out=h3b[:, (i % 2) * RPC:(i % 2 + 1) * RPC],
                                     in0=sgb, in1=sub, op=A.mult), s_ve)
            ct.wait(sy, s_ve)
            ct.wait(sy, s_h3)
            ct.bump(sy.dma_start(out=h3D[i], in_=h3b[:, (i % 2) * RPC:(i % 2 + 1) * RPC]), s_h3, 16)
        for i in range(86):
            gu_pair(i)

        for rbh in range(2):
            ct.wait(gp, s_h3)
            ct.wait(gp, s_pe)
            ct.bump(gp.dma_start(
                out=actT[:, 0:64 * 512].rearrange("p (t c) -> p t c", t=64),
                in_=h3D[0:64, :, rbh * 512:(rbh + 1) * 512].rearrange("t p c -> p t c")), s_ld, 16)
            ct.bump(gp.dma_start(
                out=xbf[:, 0:22 * 512].rearrange("p (t c) -> p t c", t=22),
                in_=h3D[64:86, :, rbh * 512:(rbh + 1) * 512].rearrange("t p c -> p t c")), s_ld, 16)

            def h3t(t):
                return (actT[:, t * 512:(t + 1) * 512] if t < 64
                        else xbf[:, (t - 64) * 512:(t - 63) * 512])

            def wait_xa(ot):
                ct.wait(sy, s_rx)
                ct.bump(sy.dma_start(out=xrf[:, (ot % 2) * RPC + rbh * 512:(ot % 2) * RPC + (rbh + 1) * 512],
                                     in_=xaD[ot * 128:(ot + 1) * 128, rbh * 512:(rbh + 1) * 512]),
                        s_rx, 16)
                ct.wait(sy, s_rx2)
                ct.bump(sy.dma_start(out=xrf[:, (ot % 2) * RPC + (1 - rbh) * 512:(ot % 2) * RPC + (2 - rbh) * 512],
                                     in_=x2T[ot * 128:(ot + 1) * 128, rbh * 512:(rbh + 1) * 512]),
                        s_rx2, 16)

            def evac_dn(ot, rb, p):
                ct.wait(ve, q["pe"])
                ct.wait(ve, s_rx)
                ct.wait(ve, s_rx2)
                ct.wait(ve, s_st)
                ofb = outf[:, (ot % 2) * RPC + rbh * 512:(ot % 2) * RPC + (rbh + 1) * 512]
                xab = xrf[:, (ot % 2) * RPC + rbh * 512:(ot % 2) * RPC + (rbh + 1) * 512]
                x2b = xrf[:, (ot % 2) * RPC + (1 - rbh) * 512:(ot % 2) * RPC + (2 - rbh) * 512]
                ve.tensor_tensor(out=ofb, in0=p, in1=xab, op=A.add)
                ve.drain()
                ct.bump(ve.tensor_tensor(out=ofb, in0=ofb, in1=x2b, op=A.add), s_ve)
                ct.wait(sy, s_ve)
                ct.wait(sy, s_st)
                ct.bump(sy.dma_start(out=x3T[ot * 128:(ot + 1) * 128, rbh * 512:(rbh + 1) * 512],
                                     in_=ofb), s_st, 16)

            ct.wait(pe, s_ld)
            _emit_fm_gemm(nc, ct, q, dn_st, 86, list(range(NDT)),
                          lambda t, rb: h3t(t),
                          wtl, [ps[:, 0:512]], evac_dn, s_w,
                          nrb=1, wait_before=wait_xa, dbuf=False)
        ct.wait(gp, s_st)
    return nc




_L1ATTN_CACHE = {}


def run_l1attn(x2_full, ln1_w, ln2_w, qkv, o, gu, dn):
    if "nc" not in _L1ATTN_CACHE:
        _L1ATTN_CACHE["nc"] = build_l1attn()
    nc = _L1ATTN_CACHE["nc"]
    qkv_qw = np.ascontiguousarray(qkv[0][:, :512])
    qkv_qz = np.ascontiguousarray(qkv[1][:, :512])
    qkv_sc = np.ascontiguousarray(qkv[2][:, :D])
    common = dict(
        ln1t=np.ascontiguousarray(ln1_w.reshape(NDT, 128).T),
        ln2t=np.ascontiguousarray(ln2_w.reshape(NDT, 128).T),
        qkv_qw=qkv_qw, qkv_qz=qkv_qz, qkv_sc=qkv_sc,
        o_qw=o[0], o_qz=o[1], o_sc=o[2],
        gu_qw=gu[0], gu_qz=gu[1], gu_sc=gu[2],
        dn_qw=dn[0], dn_qz=dn[1], dn_sc=dn[2],
    )
    xsh = _shard_xT(x2_full)
    in_maps = [dict(x2T=xsh[c], **common) for c in range(NCORES)]
    res = run_bass_kernel_spmd(nc, in_maps, core_ids=list(range(NCORES)))
    return _gather_xT([r["x3T"] for r in res.results])




VIS = 1228


def build_l1ffn():
    nc = bass.Bass("TRN2", target_bir_lowering=False)
    x3T = nc.dram_tensor("x3T", [D, RPC], F32, kind="ExternalInput").ap()
    x3B = nc.dram_tensor("x3B", [S, D], F32, kind="ExternalInput").ap()
    gwT = nc.dram_tensor("gwT", [1, 3 * (D + 1)], F32, kind="ExternalInput").ap()
    gwt = nc.dram_tensor("gwt", [128, 3 * NDT], F32, kind="ExternalInput").ap()
    gbias = nc.dram_tensor("gbias", [1, 3], F32, kind="ExternalInput").ap()
    vmask = nc.dram_tensor("vmask", [128, NDT], F32, kind="ExternalInput").ap()
    w1 = nc.dram_tensor("w1", [3, D, D], F32, kind="ExternalInput").ap()
    w2 = nc.dram_tensor("w2", [3, D, D], F32, kind="ExternalInput").ap()
    b1t = nc.dram_tensor("b1t", [3, 128, NDT], F32, kind="ExternalInput").ap()
    b2t = nc.dram_tensor("b2t", [3, 128, NDT], F32, kind="ExternalInput").ap()
    x4T = nc.dram_tensor("x4T", [D, RPC], F32, kind="ExternalOutput").ap()
    accD = nc.dram_tensor("accD", [D, RPC], F32, kind="Internal").ap()

    ct = Ct()
    with ExitStack() as ctx:
        xbf = _sb(ctx, nc, "xbf", [128, NDT * RPC], BF16)
        s_ld = ctx.enter_context(nc.semaphore("s_ld"))
        s_pe = ctx.enter_context(nc.semaphore("s_pe"))
        s_ve = ctx.enter_context(nc.semaphore("s_ve"))
        s_se = ctx.enter_context(nc.semaphore("s_se"))
        s_w = ctx.enter_context(nc.semaphore("s_w"))
        s_rx = ctx.enter_context(nc.semaphore("s_rx"))
        s_st = ctx.enter_context(nc.semaphore("s_st"))
        s_gp = ctx.enter_context(nc.semaphore("s_gp"))
        sm = dict(pe=s_pe, ve=s_ve, se=s_se)
        blk = ctx.enter_context(nc.Block())
        pe, ve, se, gp, sy = nc.tensor, nc.vector, nc.scalar, nc.gpsimd, nc.sync

        onec = _sb(ctx, nc, "onec", [128, 1], BF16)
        oner = _sb(ctx, nc, "oner", [1, 128], BF16)
        sm8 = _sb(ctx, nc, "sm8", [128, 64], F32)
        smv = lambda i: sm8[0:1, i:i + 1]
        ps = _ps(ctx, nc, "ps", [128, 8 * 512])

        gp.memset(onec, 1.0)
        ct.bump(gp.memset(oner, 1.0), s_gp, 1)

        with ExitStack() as p0:
            xb = _sb(p0, nc, "xb", [128, 2 * D], F32)
            sqb = _sb(p0, nc, "sqb", [128, D], BF16)
            rp = _sb(p0, nc, "rp", [128, NDT], F32)
            vmsk = _sb(p0, nc, "vmsk", [128, NDT], F32)
            h0b = _sb(p0, nc, "h0b", [1, D], F32)
            gwsb = _sb(p0, nc, "gwsb", [1, 3 * (D + 1)], F32)
            stt0 = _sb(p0, nc, "stt0", [128, 8], F32)
            gbsb = _sb(p0, nc, "gbsb", [1, 3], F32)
            ct.bump(gp.dma_start(out=vmsk, in_=vmask), s_ld, 16)
            ct.bump(gp.dma_start(out=gwsb, in_=gwT), s_ld, 16)
            ct.bump(gp.dma_start(out=gbsb, in_=gbias), s_ld, 16)
            for r in range(NDT):
                bb = r % 2
                ct.wait(sy, s_ve)
                ct.wait(sy, s_se)
                ct.bump(sy.dma_start(out=xb[:, bb * D:(bb + 1) * D],
                                     in_=x3B[r * 128:(r + 1) * 128, :]), s_rx, 16)
                xt = xb[:, bb * D:(bb + 1) * D]
                ct.wait(ve, s_rx)
                ve.tensor_reduce(out=stt0[:, 0:1], in_=xt, axis=mybir.AxisListType.X, op=A.add)
                ve.drain()
                ct.bump(ve.tensor_scalar(out=stt0[:, 0:1], in0=stt0[:, 0:1],
                                         scalar1=1.0 / D, scalar2=None, op0=A.mult), s_ve)
                ct.wait(se, s_rx)
                ct.wait(se, s_ve)
                ct.bump(se.activation(out=sqb, in_=xt, func=AF.Square,
                                      accum_out=stt0[:, 1:2]), s_se)
                ct.wait(ve, s_se)
                ve.drain()
                ve.tensor_scalar(out=stt0[:, 2:3], in0=stt0[:, 1:2], scalar1=1.0 / D,
                                 scalar2=None, op0=A.mult)
                ve.drain()
                ve.tensor_tensor(out=stt0[:, 3:4], in0=stt0[:, 0:1], in1=stt0[:, 0:1], op=A.mult)
                ve.drain()
                ve.tensor_tensor(out=stt0[:, 2:3], in0=stt0[:, 2:3], in1=stt0[:, 3:4], op=A.subtract)
                ve.drain()
                ve.tensor_scalar(out=stt0[:, 2:3], in0=stt0[:, 2:3], scalar1=1e-5,
                                 scalar2=None, op0=A.add)
                ve.drain()
                ct.bump(ve.reciprocal(out=stt0[:, 3:4], in_=stt0[:, 2:3]), s_ve)
                ct.wait(se, s_ve)
                ct.bump(se.activation(out=stt0[:, 4:5], in_=stt0[:, 3:4], func=AF.Sqrt), s_se)
                ct.wait(ve, s_se)
                ve.tensor_scalar(out=sqb, in0=xt, scalar1=stt0[:, 0:1], scalar2=None,
                                 op0=A.subtract)
                ve.drain()
                ve.tensor_reduce(out=stt0[:, 5:6], in_=sqb, axis=mybir.AxisListType.X,
                                 op=A.add, apply_absolute_value=True)
                ve.drain()
                ct.bump(ve.tensor_tensor(out=rp[:, r:r + 1], in0=stt0[:, 5:6],
                                         in1=stt0[:, 4:5], op=A.mult), s_ve)
                if r == 0:
                    ve.drain()
                    ve.tensor_scalar(out=h0b, in0=xt[0:1, :], scalar1=stt0[0:1, 0:1],
                                     scalar2=None, op0=A.subtract)
                    ve.drain()
                    ct.bump(ve.tensor_scalar(out=h0b, in0=h0b, scalar1=stt0[0:1, 4:5],
                                             scalar2=None, op0=A.mult), s_ve)
            ve.drain()
            ct.wait(ve, s_ld)
            vis2 = _sb(p0, nc, "vis2", [128, 2 * NDT], F32)
            ve.tensor_tensor(out=vis2[:, 0:NDT], in0=rp, in1=vmsk, op=A.mult)
            ve.drain()
            ve.tensor_tensor(out=vis2[:, NDT:2 * NDT], in0=rp, in1=vis2[:, 0:NDT], op=A.subtract)
            ve.drain()
            ve.tensor_reduce(out=sm8[:, 8:9], in_=vis2[:, 0:NDT], axis=mybir.AxisListType.X, op=A.add)
            ct.bump(ve.tensor_reduce(out=sm8[:, 9:10], in_=vis2[:, NDT:2 * NDT],
                                     axis=mybir.AxisListType.X, op=A.add), s_ve)
            vb = _sb(p0, nc, "vb", [128, 2], BF16)
            ct.wait(ve, s_ve, ct.val(s_ve))
            ve.drain()
            ct.bump(ve.tensor_copy(out=vb, in_=sm8[:, 8:10]), s_ve)
            ct.wait(pe, s_ve)
            ct.wait(pe, s_gp)
            ii = pe.matmul(ps[0:1, 0:2], lhsT=onec, rhs=vb, start=True, stop=True)
            ct.bump(ii, s_pe)
            ct.wait(ve, s_pe)
            ve.tensor_scalar(out=smv(0), in0=ps[0:1, 0:1], scalar1=1.0 / (VIS * D),
                             scalar2=None, op0=A.mult)
            ve.tensor_scalar(out=smv(1), in0=ps[0:1, 1:2], scalar1=1.0 / ((S - VIS) * D),
                             scalar2=None, op0=A.mult)
            ve.drain()
            ve.tensor_scalar(out=smv(2), in0=smv(0), scalar1=1e-8, scalar2=None, op0=A.add)
            ve.drain()
            ve.reciprocal(out=smv(3), in_=smv(2))
            ve.drain()
            ct.bump(ve.tensor_tensor(out=smv(4), in0=smv(1), in1=smv(3), op=A.mult), s_ve)
            ct.wait(ve, s_ld)
            for e in range(3):
                ve.drain()
                ve.tensor_tensor(out=sqb[0:1, 0:D], in0=h0b,
                                 in1=gwsb[0:1, e * (D + 1):e * (D + 1) + D], op=A.mult)
                ve.drain()
                ve.tensor_reduce(out=smv(8 + e), in_=sqb[0:1, 0:D],
                                 axis=mybir.AxisListType.X, op=A.add)
            ve.drain()
            for e in range(3):
                ve.tensor_scalar(out=smv(12 + e), in0=smv(4), scalar1=gwsb[0:1, e * (D + 1) + D:e * (D + 1) + D + 1],
                                 scalar2=None, op0=A.mult)
            ve.drain()
            for e in range(3):
                ve.tensor_tensor(out=smv(8 + e), in0=smv(8 + e), in1=smv(12 + e), op=A.add)
            ve.drain()
            for e in range(3):
                ve.tensor_tensor(out=smv(8 + e), in0=smv(8 + e), in1=gbsb[0:1, e:e + 1], op=A.add)
            ve.drain()
            k = 16
            for e in range(3):
                for e2 in range(3):
                    if e2 == e:
                        continue
                    ve.tensor_tensor(out=smv(k), in0=smv(8 + e2), in1=smv(8 + e), op=A.is_gt)
                    k += 1
            ve.drain()
            for e in range(3):
                for e2 in range(e):
                    ve.tensor_tensor(out=smv(k), in0=smv(8 + e2), in1=smv(8 + e), op=A.is_equal)
                    k += 1
            ve.drain()
            ve.tensor_tensor(out=smv(25), in0=smv(16), in1=smv(17), op=A.add)
            ve.tensor_tensor(out=smv(26), in0=smv(18), in1=smv(19), op=A.add)
            ve.tensor_tensor(out=smv(27), in0=smv(20), in1=smv(21), op=A.add)
            ve.drain()
            ve.tensor_tensor(out=smv(26), in0=smv(26), in1=smv(22), op=A.add)
            ve.tensor_tensor(out=smv(28), in0=smv(23), in1=smv(24), op=A.add)
            ve.drain()
            ve.tensor_tensor(out=smv(27), in0=smv(27), in1=smv(28), op=A.add)
            ve.drain()
            for e in range(3):
                ve.tensor_scalar(out=smv(32 + e), in0=smv(25 + e), scalar1=0.0,
                                 scalar2=None, op0=A.is_equal)
                ve.tensor_scalar(out=smv(36 + e), in0=smv(25 + e), scalar1=1.0,
                                 scalar2=None, op0=A.is_equal)
            ct.bump(ve.drain(), s_ve)
        p0_done = ct.val(s_ve)

        actT = _sb(ctx, nc, "actT", [128, NDT * RPC], BF16)
        wtl = _sb(ctx, nc, "wtl", [128, 2 * NDT * 128], BF16)
        sq = _sb(ctx, nc, "sq", [128, 2 * RPC], BF16)
        st = _sb(ctx, nc, "st", [1, 4 * RPC], F32)
        stb = _sb(ctx, nc, "stb", [1, 2 * RPC], BF16)
        rstd_bc = _sb(ctx, nc, "rstd_bc", [128, RPC], BF16)
        mr_bc = _sb(ctx, nc, "mr_bc", [128, RPC], BF16)
        gwtl = _sb(ctx, nc, "gwtl", [128, 3 * NDT], BF16)
        biast = _sb(ctx, nc, "biast", [128, 6 * NDT], F32)
        web = _sb(ctx, nc, "web", [128, 3 * RPC], BF16)
        gb2 = _sb(ctx, nc, "gb2", [1, 3], F32)
        p1a = ExitStack()
        lgf = _sb(p1a, nc, "lgf", [1, 3 * RPC], F32)
        egf = lgf
        tvf = _sb(p1a, nc, "tvf", [1, 4 * RPC], F32)

        ct.wait(gp, s_ve, p0_done)
        ct.wait(gp, s_se)
        ct.wait(gp, s_pe)
        ct.bump(gp.dma_start(out=xbf.rearrange("p (t c) -> p t c", t=NDT),
                             in_=x3T.rearrange("(t p) c -> p t c", p=128)), s_ld, 16)
        ct.bump(gp.dma_start(out=gwtl, in_=gwt), s_ld, 16)
        ct.bump(gp.dma_start(out=biast[:, 0:3 * NDT].rearrange("p (e t) -> p e t", e=3),
                             in_=b1t.rearrange("e p t -> p e t")), s_ld, 16)
        ct.bump(gp.dma_start(out=biast[:, 3 * NDT:6 * NDT].rearrange("p (e t) -> p e t", e=3),
                             in_=b2t.rearrange("e p t -> p e t")), s_ld, 16)
        ct.bump(gp.dma_start(out=gb2, in_=gbias), s_ld, 16)

        ct.wait(pe, s_ld)
        ct.wait(se, s_ld)
        ct.wait(ve, s_ld)
        ps_m = ps[0:1, 4 * 512:4 * 512 + RPC]
        ps_v = ps[0:1, 6 * 512:6 * 512 + RPC]
        ps_bc = ps[:, 4 * 512:8 * 512]
        emit_ln(nc, ct, sm, xbf, RPC, onec, oner, sq, ps_m, ps_v, ps_bc, st,
                rstd_bc, mr_bc, xbf, 1e-5, rms=False, stb=stb)
        xn = lambda t: xbf[:, t * RPC:(t + 1) * RPC]
        ct.wait(pe, s_ve)

        for e in range(3):
            for rb in range(2):
                pb = ps[e * 32:e * 32 + 1, rb * 512:(rb + 1) * 512]
                for t in range(NDT):
                    ii = pe.matmul(pb, lhsT=gwtl[:, (e * NDT + t):(e * NDT + t) + 1],
                                   rhs=xn(t)[:, rb * 512:(rb + 1) * 512],
                                   start=(t == 0), stop=(t == NDT - 1))
        ct.bump(ii, s_pe)
        ct.wait(ve, s_pe)
        for e in range(3):
            ve.tensor_copy(out=lgf[0:1, e * RPC:e * RPC + 512], in_=ps[e * 32:e * 32 + 1, 0:512])
            ve.tensor_copy(out=lgf[0:1, e * RPC + 512:(e + 1) * RPC], in_=ps[e * 32:e * 32 + 1, 512:1024])
        ve.drain()
        ct.wait(ve, s_ve, p0_done)
        for e in range(3):
            ve.tensor_tensor(out=sm8[0:1, 44 + e:45 + e], in0=sm8[0:1, 12 + e:13 + e],
                             in1=gb2[0:1, e:e + 1], op=A.add)
        ve.drain()
        for e in range(3):
            ve.tensor_scalar(out=lgf[0:1, e * RPC:(e + 1) * RPC], in0=lgf[0:1, e * RPC:(e + 1) * RPC],
                             scalar1=sm8[0:1, 44 + e:45 + e], scalar2=None, op0=A.add)
        ct.bump(ve.drain(), s_ve)
        ct.wait(se, s_ve)
        ct.bump(se.activation(out=egf, in_=lgf, func=AF.Exp), s_se)
        ct.wait(ve, s_se)
        t0 = lambda i: tvf[:, i * RPC:(i + 1) * RPC]
        ve.tensor_tensor(out=t0(0), in0=egf[0:1, RPC:2 * RPC], in1=egf[0:1, 2 * RPC:3 * RPC], op=A.max)
        ve.tensor_tensor(out=t0(1), in0=egf[0:1, RPC:2 * RPC], in1=egf[0:1, 2 * RPC:3 * RPC], op=A.min)
        ve.tensor_tensor(out=t0(2), in0=egf[0:1, RPC:2 * RPC], in1=egf[0:1, 2 * RPC:3 * RPC], op=A.add)
        ve.drain()
        ve.drain()
        ve.tensor_tensor(out=t0(3), in0=egf[0:1, 0:RPC], in1=t0(0), op=A.max)
        ve.drain()
        ve.tensor_tensor(out=t0(0), in0=egf[0:1, 0:RPC], in1=t0(1), op=A.min)
        ve.tensor_tensor(out=t0(2), in0=egf[0:1, 0:RPC], in1=t0(2), op=A.add)
        ve.drain()
        ve.tensor_tensor(out=t0(1), in0=t0(2), in1=t0(3), op=A.subtract)
        ve.drain()
        ve.tensor_tensor(out=t0(1), in0=t0(1), in1=t0(0), op=A.subtract)
        ve.drain()
        ve.reciprocal(out=t0(0), in_=t0(2))
        ve.drain()
        ve.tensor_tensor(out=t0(3), in0=t0(3), in1=t0(0), op=A.mult)
        ve.tensor_tensor(out=t0(1), in0=t0(1), in1=t0(0), op=A.mult)
        ve.drain()
        ve.tensor_tensor(out=t0(2), in0=t0(3), in1=t0(1), op=A.add)
        ve.drain()
        ve.tensor_scalar(out=t0(2), in0=t0(2), scalar1=1e-8, scalar2=None, op0=A.add)
        ve.drain()
        ve.reciprocal(out=t0(2), in_=t0(2))
        ve.drain()
        ve.tensor_tensor(out=t0(3), in0=t0(3), in1=t0(2), op=A.mult)
        ve.tensor_tensor(out=t0(0), in0=t0(1), in1=t0(2), op=A.mult)
        ve.drain()
        for e in range(3):
            ve.tensor_scalar(out=stb[:, 0:RPC], in0=t0(3), scalar1=sm8[0:1, 32 + e:33 + e],
                             scalar2=None, op0=A.mult)
            ve.tensor_scalar(out=stb[:, RPC:2 * RPC], in0=t0(0), scalar1=sm8[0:1, 36 + e:37 + e],
                             scalar2=None, op0=A.mult)
            ve.drain()
            ct.bump(ve.tensor_tensor(out=stb[:, 0:RPC], in0=stb[:, 0:RPC],
                                     in1=stb[:, RPC:2 * RPC], op=A.add), s_ve)
            ct.wait(pe, s_ve)
            pe.matmul(ps_bc[:, 0:512], lhsT=oner, rhs=stb[:, 0:512], start=True, stop=True)
            ii = pe.matmul(ps_bc[:, 512:1024], lhsT=oner, rhs=stb[:, 512:1024], start=True, stop=True)
            ct.bump(ii, s_pe)
            ct.wait(ve, s_pe)
            ct.bump(ve.tensor_copy(out=web[:, e * RPC:(e + 1) * RPC], in_=ps_bc[:, 0:RPC]), s_ve)

        p1a.close()
        xrf = _sb(ctx, nc, "xrf", [128, RPC], F32)
        outf = _sb(ctx, nc, "outf", [128, RPC], F32)
        ACTF = [AF.Tanh, AF.Gelu, AF.Silu]
        for e in range(3):
            def load_w(src, ft, buf):
                ct.wait(gp, s_pe)
                ct.wait(gp, s_w)
                ct.bump(gp.dma_start(
                    out=wtl[:, buf * NDT * 128:(buf + 1) * NDT * 128].rearrange("p (t m) -> p t m", t=NDT),
                    in_=src[:, ft * 128:(ft + 1) * 128].rearrange("(t p) c -> p t c", p=128)), s_w, 16)
            w_base = ct.val(s_w)
            load_w(w1[e], 0, 0)
            for ft in range(NDT):
                bb = ft % 2
                if ft + 1 < NDT:
                    load_w(w1[e], ft + 1, (ft + 1) % 2)
                ct.wait(pe, s_w, w_base + 16 * (ft + 1))
                for rb in range(2):
                    pbank = ps[:, rb * 512:(rb + 1) * 512]
                    ct.wait(pe, s_se, ct.val(s_se))
                    ct.wait(pe, s_ve, ct.val(s_ve))
                    for t in range(NDT):
                        ii = pe.matmul(pbank, lhsT=wtl[:, bb * NDT * 128 + t * 128:bb * NDT * 128 + (t + 1) * 128],
                                       rhs=xn(t)[:, rb * 512:(rb + 1) * 512],
                                       start=(t == 0), stop=(t == NDT - 1))
                    ct.bump(ii, s_pe)
                    ct.wait(se, s_pe)
                    ct.bump(se.activation(out=actT[:, ft * RPC + rb * 512:ft * RPC + (rb + 1) * 512],
                                          in_=pbank, func=ACTF[e],
                                          bias=biast[:, e * NDT + ft:e * NDT + ft + 1]), s_se)
            ct.wait(pe, s_se)
            w_base = ct.val(s_w)
            load_w(w2[e], 0, 0)
            at = lambda t: actT[:, t * RPC:(t + 1) * RPC]
            for ot in range(NDT):
                bb = ot % 2
                if ot + 1 < NDT:
                    load_w(w2[e], ot + 1, (ot + 1) % 2)
                ct.wait(pe, s_w, w_base + 16 * (ot + 1))
                ct.wait(sy, s_ve)
                ct.wait(sy, s_rx)
                ct.wait(sy, s_st)
                src = x3T if e == 0 else accD
                ct.bump(sy.dma_start(out=xrf,
                                     in_=src[ot * 128:(ot + 1) * 128, :]), s_rx, 16)
                for rb in range(2):
                    pbank = ps[:, rb * 512:(rb + 1) * 512]
                    ct.wait(pe, s_ve, ct.val(s_ve))
                    for t in range(NDT):
                        ii = pe.matmul(pbank, lhsT=wtl[:, bb * NDT * 128 + t * 128:bb * NDT * 128 + (t + 1) * 128],
                                       rhs=at(t)[:, rb * 512:(rb + 1) * 512],
                                       start=(t == 0), stop=(t == NDT - 1))
                    ct.bump(ii, s_pe)
                    ct.wait(ve, s_pe)
                    ct.wait(ve, s_rx)
                    ct.wait(ve, s_st)
                    ofb = outf[:, rb * 512:(rb + 1) * 512]
                    xrb = xrf[:, rb * 512:(rb + 1) * 512]
                    ve.tensor_scalar(out=ofb, in0=pbank,
                                     scalar1=biast[:, 3 * NDT + e * NDT + ot:3 * NDT + e * NDT + ot + 1],
                                     scalar2=None, op0=A.add)
                    ve.drain()
                    ve.tensor_tensor(out=ofb, in0=ofb,
                                     in1=web[:, e * RPC + rb * 512:e * RPC + (rb + 1) * 512], op=A.mult)
                    ve.drain()
                    ct.bump(ve.tensor_tensor(out=ofb, in0=ofb, in1=xrb, op=A.add), s_ve)
                ct.wait(sy, s_ve)
                ct.wait(sy, s_st)
                dst = x4T if e == 2 else accD
                ct.bump(sy.dma_start(out=dst[ot * 128:(ot + 1) * 128, :],
                                     in_=outf), s_st, 16)
        ct.wait(gp, s_st)
    return nc


_L1FFN_CACHE = {}


def _pad_gwT(gate_w):
    return np.ascontiguousarray(gate_w.T.reshape(1, 3 * (D + 1)))


def run_l1ffn(x3_full, gate_w, gate_b, w1, b1, w2, b2):
    if "nc" not in _L1FFN_CACHE:
        _L1FFN_CACHE["nc"] = build_l1ffn()
    nc = _L1FFN_CACHE["nc"]
    vm = np.zeros((128, NDT), np.float32)
    for r in range(NDT):
        rows = np.arange(r * 128, (r + 1) * 128)
        vm[:, r] = (rows < VIS).astype(np.float32)
    common = dict(
        gwT=_pad_gwT(gate_w),
        gwt=np.ascontiguousarray(gate_w[:D, :].reshape(NDT, 128, 3).transpose(1, 2, 0).reshape(128, 3 * NDT)),
        gbias=gate_b.reshape(1, 3),
        vmask=vm, w1=w1, w2=w2,
        b1t=np.ascontiguousarray(b1.reshape(3, NDT, 128).transpose(0, 2, 1)),
        b2t=np.ascontiguousarray(b2.reshape(3, NDT, 128).transpose(0, 2, 1)),
    )
    xsh = _shard_xT(x3_full)
    in_maps = []
    for c in range(NCORES):
        in_maps.append(dict(x3T=xsh[c], x3B=np.ascontiguousarray(x3_full[c // 4]), **common))
    res = run_bass_kernel_spmd(nc, in_maps, core_ids=list(range(NCORES)))
    return _gather_xT([r["x4T"] for r in res.results])




NEGC = -600.0


def build_l0attn():
    nc = bass.Bass("TRN2", target_bir_lowering=False)
    x0T = nc.dram_tensor("x0T", [D, RPC], F32, kind="ExternalInput").ap()
    x0TB = nc.dram_tensor("x0TB", [D, S], F32, kind="ExternalInput").ap()
    wqT = nc.dram_tensor("wqT", [D, D], F32, kind="ExternalInput").ap()
    wkT = nc.dram_tensor("wkT", [D, KVH * HD], F32, kind="ExternalInput").ap()
    wvT = nc.dram_tensor("wvT", [D, KVH * HD], F32, kind="ExternalInput").ap()
    woT = nc.dram_tensor("woT", [D, D], F32, kind="ExternalInput").ap()
    p1w = nc.dram_tensor("p1w", [NH, HD, HD], F32, kind="ExternalInput").ap()
    p2w = nc.dram_tensor("p2w", [NH, HD, HD], F32, kind="ExternalInput").ap()
    hb1 = nc.dram_tensor("hb1", [HD, NH], F32, kind="ExternalInput").ap()
    hb2 = nc.dram_tensor("hb2", [HD, NH], F32, kind="ExternalInput").ap()
    cosO = nc.dram_tensor("cosO", [HD, RPC], F32, kind="ExternalInput").ap()
    sinO = nc.dram_tensor("sinO", [HD, RPC], F32, kind="ExternalInput").ap()
    cosB = nc.dram_tensor("cosB", [HD, S], F32, kind="ExternalInput").ap()
    sinB = nc.dram_tensor("sinB", [HD, S], F32, kind="ExternalInput").ap()
    rotT = nc.dram_tensor("rotT", [HD, HD], F32, kind="ExternalInput").ap()
    caus = nc.dram_tensor("caus", [NSLOT, 128, 512], F32, kind="ExternalInput").ap()
    x1T = nc.dram_tensor("x1T", [D, RPC], F32, kind="ExternalOutput").ap()

    krD = nc.dram_tensor("krD", [KVH, HD, S], BF16, kind="Internal").ap()
    vD = nc.dram_tensor("vD", [S, KVH * HD], BF16, kind="Internal").ap()
    khD = nc.dram_tensor("khD", [NH, HD, S], BF16, kind="Internal").ap()
    qrD = nc.dram_tensor("qrD", [NH, HD, RPC], BF16, kind="Internal").ap()
    qhD = nc.dram_tensor("qhD", [NH, HD, RPC], BF16, kind="Internal").ap()
    atD = nc.dram_tensor("atD", [NH, HD, RPC], BF16, kind="Internal").ap()

    ct = Ct()
    with ExitStack() as ctx:
        xbf = _sb(ctx, nc, "xbf", [128, NDT * RPC], BF16)
        sq = _sb(ctx, nc, "sq", [128, 2 * RPC], BF16)
        st = _sb(ctx, nc, "st", [1, 4 * RPC], F32)
        stb = _sb(ctx, nc, "stb", [1, 2 * RPC], BF16)
        rstd_bc = _sb(ctx, nc, "rstd_bc", [128, RPC], BF16)
        mr_bc = _sb(ctx, nc, "mr_bc", [128, RPC], BF16)
        onec = _sb(ctx, nc, "onec", [128, 1], BF16)
        oner = _sb(ctx, nc, "oner", [1, 128], BF16)
        wtl = _sb(ctx, nc, "wtl", [128, 2 * NDT * 128], BF16)
        rotb = _sb(ctx, nc, "rotb", [128, HD], BF16)
        idb = _sb(ctx, nc, "idb", [128, 128], BF16)
        iwork = _sb(ctx, nc, "iwork", [128, 128], F32)
        irow = _sb(ctx, nc, "irow", [128, 1], F32)
        hbb = _sb(ctx, nc, "hbb", [128, 2 * NH], F32)
        p12 = _sb(ctx, nc, "p12", [128, 2 * HD], BF16)
        s_ld = ctx.enter_context(nc.semaphore("s_ld"))
        s_pe = ctx.enter_context(nc.semaphore("s_pe"))
        s_ve = ctx.enter_context(nc.semaphore("s_ve"))
        s_se = ctx.enter_context(nc.semaphore("s_se"))
        s_w = ctx.enter_context(nc.semaphore("s_w"))
        s_rx = ctx.enter_context(nc.semaphore("s_rx"))
        s_px = ctx.enter_context(nc.semaphore("s_px"))
        s_c2 = ctx.enter_context(nc.semaphore("s_c2"))
        s_st = ctx.enter_context(nc.semaphore("s_st"))
        s_sc = ctx.enter_context(nc.semaphore("s_sc"))
        s_gp = ctx.enter_context(nc.semaphore("s_gp"))
        sm = dict(pe=s_pe, ve=s_ve, se=s_se)
        blk = ctx.enter_context(nc.Block())
        pe, ve, se, gp, sy = nc.tensor, nc.vector, nc.scalar, nc.gpsimd, nc.sync

        gp.memset(onec, 1.0)
        gp.memset(oner, 1.0)
        ct.bump(gp.iota(iwork, pattern=[[1, 128]], base=0, channel_multiplier=0,
                        allow_small_or_imprecise_dtypes=True), s_gp)
        ct.bump(gp.iota(irow, pattern=[[0, 1]], base=0, channel_multiplier=1,
                        allow_small_or_imprecise_dtypes=True), s_gp)
        ct.wait(ve, s_gp)
        ct.bump(ve.tensor_scalar(out=idb, in0=iwork, scalar1=irow[:, 0:1], scalar2=None,
                                 op0=A.is_equal), s_ve)
        ct.bump(gp.dma_start(out=rotb, in_=rotT), s_ld, 16)
        ct.bump(gp.dma_start(out=hbb[:, 0:NH], in_=hb1), s_ld, 16)
        ct.bump(gp.dma_start(out=hbb[:, NH:2 * NH], in_=hb2), s_ld, 16)
        ct.bump(gp.dma_start(out=xbf.rearrange("p (t c) -> p t c", t=NDT),
                             in_=x0T.rearrange("(t p) c -> p t c", p=128)), s_ld, 16)

        ps = _ps(ctx, nc, "ps", [128, 7 * 512])
        psT = _ps(ctx, nc, "psT", [128, 1024], BF16)
        ps_m = ps[0:1, 3 * 512:3 * 512 + RPC]
        ps_v = ps[0:1, 5 * 512:5 * 512 + RPC]
        ps_bc = ps[:, 3 * 512:7 * 512]

        ct.wait(pe, s_gp)
        ct.wait(pe, s_ld)
        ct.wait(se, s_ld)
        emit_ln(nc, ct, sm, xbf, RPC, onec, oner, sq, ps_m, ps_v, ps_bc, st,
                rstd_bc, mr_bc, xbf, 1e-5, rms=False, stb=stb)
        xn = lambda t: xbf[:, t * RPC:(t + 1) * RPC]
        ct.wait(pe, s_ve)

        with ExitStack() as pa:
            qb = _sb(pa, nc, "qb", [128, RPC], BF16)
            rqb = _sb(pa, nc, "rqb", [128, RPC], BF16)
            qrb = _sb(pa, nc, "qrb", [128, RPC], BF16)
            u1b = _sb(pa, nc, "u1b", [128, RPC], BF16)
            cosb = _sb(pa, nc, "cosb", [128, 2 * RPC], F32)
            ct.bump(gp.dma_start(out=cosb[:, 0:RPC], in_=cosO), s_c2, 16)
            ct.bump(gp.dma_start(out=cosb[:, RPC:2 * RPC], in_=sinO), s_c2, 16)

            def loadw_q(ot, buf):
                ct.wait(gp, s_w)
                ct.wait(gp, s_pe)
                ct.bump(gp.dma_start(
                    out=wtl[:, buf * NDT * 128:(buf + 1) * NDT * 128]
                        .rearrange("p (t m) -> p t m", t=NDT),
                    in_=wqT[:, ot * 128:(ot + 1) * 128].rearrange("(t p) c -> p t c", p=128)),
                    s_w, 16)

            w_base = ct.val(s_w)
            loadw_q(0, 0)
            for h in range(NH):
                bb = h % 2
                ct.wait(ve, s_sc)
                ct.wait(se, s_sc)
                if h + 1 < NH:
                    loadw_q(h + 1, (h + 1) % 2)
                ct.wait(pe, s_w, w_base + 16 * (h + 1))
                ct.wait(gp, s_px)
                ct.wait(gp, s_pe)
                ct.bump(gp.dma_start(out=p12[:, 0:HD], in_=p1w[h]), s_px, 16)
                ct.bump(gp.dma_start(out=p12[:, HD:2 * HD], in_=p2w[h]), s_px, 16)
                for rb in range(2):
                    pb = ps[:, rb * 512:(rb + 1) * 512]
                    ct.wait(pe, s_ve, ct.val(s_ve))
                    ct.wait(pe, s_se, ct.val(s_se))
                    for t in range(NDT):
                        ii = pe.matmul(pb, lhsT=wtl[:, bb * NDT * 128 + t * 128:bb * NDT * 128 + (t + 1) * 128],
                                       rhs=xn(t)[:, rb * 512:(rb + 1) * 512],
                                       start=(t == 0), stop=(t == NDT - 1))
                    ct.bump(ii, s_pe)
                    ct.wait(se, s_pe)
                    ct.bump(se.activation(out=qb[:, rb * 512:(rb + 1) * 512], in_=pb,
                                          func=AF.Copy), s_se)
                ct.wait(pe, s_se)
                for rb in range(2):
                    ii = pe.matmul(ps[:, 1024 + rb * 512:1024 + (rb + 1) * 512], lhsT=rotb,
                                   rhs=qb[:, rb * 512:(rb + 1) * 512], start=True, stop=True)
                ct.bump(ii, s_pe)
                ct.wait(ve, s_pe)
                ct.wait(ve, s_c2)
                ve.tensor_copy(out=rqb, in_=ps[:, 1024:2048])
                ve.drain()
                ve.tensor_tensor(out=qrb, in0=qb, in1=cosb[:, 0:RPC], op=A.mult)
                ve.tensor_tensor(out=rqb, in0=rqb, in1=cosb[:, RPC:2 * RPC], op=A.mult)
                ve.drain()
                ct.bump(ve.tensor_tensor(out=qrb, in0=qrb, in1=rqb, op=A.add), s_ve)
                ct.wait(gp, s_ve)
                ct.bump(gp.dma_start(out=qrD[h], in_=qrb), s_sc, 16)
                ct.wait(pe, s_ve)
                ct.wait(pe, s_px, ct.val(s_px))
                for rb in range(2):
                    ii = pe.matmul(ps[:, rb * 512:(rb + 1) * 512], lhsT=p12[:, 0:HD],
                                   rhs=qrb[:, rb * 512:(rb + 1) * 512], start=True, stop=True)
                ct.bump(ii, s_pe)
                ct.wait(se, s_pe)
                ct.bump(se.activation(out=u1b, in_=ps[:, 0:1024], func=AF.Silu,
                                      bias=hbb[:, h:h + 1]), s_se)
                ct.wait(ve, s_se)
                ve.drain()
                ct.bump(ve.tensor_tensor(out=u1b, in0=u1b, in1=qrb, op=A.add), s_ve)
                ct.wait(pe, s_ve)
                for rb in range(2):
                    ii = pe.matmul(ps[:, 1024 + rb * 512:1024 + (rb + 1) * 512],
                                   lhsT=p12[:, HD:2 * HD],
                                   rhs=u1b[:, rb * 512:(rb + 1) * 512], start=True, stop=True)
                ct.bump(ii, s_pe)
                ct.wait(ve, s_pe)
                ve.drain()
                ve.tensor_tensor(out=qb, in0=ps[:, 1024:2048], in1=u1b, op=A.add)
                ct.bump(ve.drain(), s_ve)
                ct.wait(se, s_ve)
                ct.bump(se.activation(out=u1b, in_=qb, func=AF.Sign,
                                      bias=hbb[:, NH + h:NH + h + 1]), s_se)
                ct.wait(gp, s_se)
                ct.bump(gp.dma_start(out=qhD[h], in_=u1b), s_sc, 16)

        ct.wait(pe, s_ve)
        ct.wait(pe, s_se)

        with ExitStack() as pb2:
            QW = 1024
            cosq = _sb(pb2, nc, "cosq", [128, 2 * QW], F32)
            kvb = _sb(pb2, nc, "kvb", [128, KVH * QW], BF16)
            rkb = _sb(pb2, nc, "rkb", [128, QW], BF16)
            u1k = _sb(pb2, nc, "u1k", [128, QW], BF16)
            u2k = _sb(pb2, nc, "u2k", [128, QW], BF16)
            vob = _sb(pb2, nc, "vob", [128, RPC], BF16)
            for qtr in range(4):
                ct.wait(gp, s_ve)
                ct.wait(gp, s_se)
                ct.wait(gp, s_pe)
                ct.wait(gp, s_c2)
                ct.wait(gp, s_sc)
                ct.bump(gp.dma_start(out=xbf.rearrange("p (t c) -> p t c", t=NDT),
                                     in_=x0TB[:, qtr * QW:(qtr + 1) * QW]
                                     .rearrange("(t p) c -> p t c", p=128)), s_ld, 16)
                ct.bump(gp.dma_start(out=cosq[:, 0:QW], in_=cosB[:, qtr * QW:(qtr + 1) * QW]), s_c2, 16)
                ct.bump(gp.dma_start(out=cosq[:, QW:2 * QW], in_=sinB[:, qtr * QW:(qtr + 1) * QW]), s_c2, 16)
                ct.wait(pe, s_ld)
                ct.wait(se, s_ld)
                ct.wait(ve, s_ld)
                emit_ln(nc, ct, sm, xbf, QW, onec, oner, sq, ps_m, ps_v, ps_bc, st,
                        rstd_bc, mr_bc, xbf, 1e-5, rms=False, stb=stb)
                xq = lambda t: xbf[:, t * QW:(t + 1) * QW]
                ct.wait(pe, s_ve)
                w_base = ct.val(s_w)

                def loadw_k(kv):
                    ct.wait(gp, s_w)
                    ct.wait(gp, s_pe)
                    ct.bump(gp.dma_start(
                        out=wtl[:, (kv % 2) * NDT * 128:((kv % 2) + 1) * NDT * 128]
                            .rearrange("p (t m) -> p t m", t=NDT),
                        in_=wkT[:, kv * 128:(kv + 1) * 128].rearrange("(t p) c -> p t c", p=128)),
                        s_w, 16)

                loadw_k(0)
                for kv in range(KVH):
                    if kv + 1 < KVH:
                        loadw_k(kv + 1)
                    ct.wait(pe, s_w, w_base + 16 * (kv + 1))
                    for rb in range(2):
                        pb = ps[:, rb * 512:(rb + 1) * 512]
                        ct.wait(pe, s_ve, ct.val(s_ve))
                        ct.wait(pe, s_se, ct.val(s_se))
                        for t in range(NDT):
                            ii = pe.matmul(pb, lhsT=wtl[:, (kv % 2) * NDT * 128 + t * 128:(kv % 2) * NDT * 128 + (t + 1) * 128],
                                           rhs=xq(t)[:, rb * 512:(rb + 1) * 512],
                                           start=(t == 0), stop=(t == NDT - 1))
                        ct.bump(ii, s_pe)
                        ct.wait(se, s_pe)
                        ct.bump(se.activation(out=u1k[:, rb * 512:(rb + 1) * 512], in_=pb,
                                              func=AF.Copy), s_se)
                    ct.wait(pe, s_se)
                    for rb in range(2):
                        ii = pe.matmul(ps[:, 1024 + rb * 512:1024 + (rb + 1) * 512], lhsT=rotb,
                                       rhs=u1k[:, rb * 512:(rb + 1) * 512], start=True, stop=True)
                    ct.bump(ii, s_pe)
                    ct.wait(ve, s_pe)
                    ct.wait(ve, s_c2)
                    ve.tensor_copy(out=rkb, in_=ps[:, 1024:2048])
                    ve.drain()
                    kvt = kvb[:, kv * QW:(kv + 1) * QW]
                    ve.tensor_tensor(out=kvt, in0=u1k, in1=cosq[:, 0:QW], op=A.mult)
                    ve.tensor_tensor(out=rkb, in0=rkb, in1=cosq[:, QW:2 * QW], op=A.mult)
                    ve.drain()
                    ct.bump(ve.tensor_tensor(out=kvt, in0=kvt, in1=rkb, op=A.add), s_ve)
                    ct.wait(gp, s_ve)
                    ct.bump(gp.dma_start(out=krD[kv][:, qtr * QW:(qtr + 1) * QW], in_=kvt), s_sc, 16)
                for h in range(NH):
                    kv = h // 4
                    kvt = kvb[:, kv * QW:(kv + 1) * QW]
                    ct.wait(se, s_sc)
                    ct.wait(gp, s_px)
                    ct.wait(gp, s_pe)
                    ct.bump(gp.dma_start(out=p12[:, 0:HD], in_=p1w[h]), s_px, 16)
                    ct.bump(gp.dma_start(out=p12[:, HD:2 * HD], in_=p2w[h]), s_px, 16)
                    ct.wait(pe, s_px)
                    ct.wait(pe, s_ve, ct.val(s_ve))
                    ct.wait(pe, s_se, ct.val(s_se))
                    for rb in range(2):
                        ii = pe.matmul(ps[:, rb * 512:(rb + 1) * 512], lhsT=p12[:, 0:HD],
                                       rhs=kvt[:, rb * 512:(rb + 1) * 512], start=True, stop=True)
                    ct.bump(ii, s_pe)
                    ct.wait(se, s_pe)
                    ct.bump(se.activation(out=u1k, in_=ps[:, 0:1024], func=AF.Silu,
                                          bias=hbb[:, h:h + 1]), s_se)
                    ct.wait(ve, s_se)
                    ve.drain()
                    ct.bump(ve.tensor_tensor(out=u1k, in0=u1k, in1=kvt, op=A.add), s_ve)
                    ct.wait(pe, s_ve)
                    for rb in range(2):
                        ii = pe.matmul(ps[:, 1024 + rb * 512:1024 + (rb + 1) * 512],
                                       lhsT=p12[:, HD:2 * HD],
                                       rhs=u1k[:, rb * 512:(rb + 1) * 512], start=True, stop=True)
                    ct.bump(ii, s_pe)
                    ct.wait(ve, s_pe)
                    ve.drain()
                    ve.tensor_tensor(out=u2k, in0=ps[:, 1024:2048], in1=u1k, op=A.add)
                    ct.bump(ve.drain(), s_ve)
                    ct.wait(se, s_ve)
                    ct.bump(se.activation(out=u2k, in_=u2k, func=AF.Sign,
                                          bias=hbb[:, NH + h:NH + h + 1]), s_se)
                    ct.wait(gp, s_se)
                    ct.bump(gp.dma_start(out=khD[h][:, qtr * QW:(qtr + 1) * QW], in_=u2k), s_sc, 16)
                w_base = ct.val(s_w)

                def loadw_v(kv):
                    ct.wait(gp, s_w)
                    ct.wait(gp, s_pe)
                    ct.bump(gp.dma_start(
                        out=wtl[:, (kv % 2) * NDT * 128:((kv % 2) + 1) * NDT * 128]
                            .rearrange("p (t m) -> p t m", t=NDT),
                        in_=wvT[:, kv * 128:(kv + 1) * 128].rearrange("(t p) c -> p t c", p=128)),
                        s_w, 16)

                loadw_v(0)
                for kv in range(KVH):
                    if kv + 1 < KVH:
                        loadw_v(kv + 1)
                    ct.wait(pe, s_w, w_base + 16 * (kv + 1))
                    for rb in range(2):
                        pb = ps[:, rb * 512:(rb + 1) * 512]
                        ct.wait(pe, s_ve, ct.val(s_ve))
                        ct.wait(pe, s_se, ct.val(s_se))
                        for t in range(NDT):
                            ii = pe.matmul(pb, lhsT=wtl[:, (kv % 2) * NDT * 128 + t * 128:(kv % 2) * NDT * 128 + (t + 1) * 128],
                                           rhs=xq(t)[:, rb * 512:(rb + 1) * 512],
                                           start=(t == 0), stop=(t == NDT - 1))
                        ct.bump(ii, s_pe)
                        ct.wait(se, s_pe)
                        ct.bump(se.activation(out=u1k[:, rb * 512:(rb + 1) * 512], in_=pb,
                                              func=AF.Copy), s_se)
                    ct.wait(pe, s_se)
                    for b8 in range(8):
                        if b8 == 4:
                            ct.bump(ii, s_pe)
                            ct.wait(ve, s_pe)
                            ct.wait(ve, s_sc)
                            ct.bump(ve.tensor_copy(out=vob[:, 0:512], in_=psT[:, 0:512]), s_ve)
                            ct.wait(pe, s_ve, ct.val(s_ve))
                        ii = pe.transpose(psT[:, (b8 % 4) * 128:(b8 % 4 + 1) * 128],
                                          in_=u1k[:, b8 * 128:(b8 + 1) * 128], identity=idb)
                    ct.bump(ii, s_pe)
                    ct.wait(ve, s_pe)
                    ct.bump(ve.tensor_copy(out=vob[:, 512:1024], in_=psT[:, 512:1024]
                                           if False else psT[:, 0:512]), s_ve)
                    ct.wait(gp, s_ve)
                    ct.bump(gp.dma_start(
                        out=vD[qtr * QW:(qtr + 1) * QW, kv * 128:(kv + 1) * 128]
                            .rearrange("(b p) m -> p b m", p=128),
                        in_=vob.rearrange("p (b m) -> p b m", m=128)), s_sc, 16)
        ct.wait(pe, s_ve)
        ct.wait(pe, s_se)
        ct.wait(pe, s_sc)

        SIMW = 18432
        simb = xbf[:, 0:SIMW]
        soff = [512 * (s * (s + 1) // 2) for s in range(NSLOT + 1)]
        escb = xbf[:, SIMW:SIMW + 4096]
        eslb = xbf[:, SIMW + 4096:SIMW + 8192]
        etb = xbf[:, SIMW + 8192:SIMW + 12288]
        with ExitStack() as pc:
            khb = _sb(pc, nc, "khb", [128, S], BF16)
            krb = _sb(pc, nc, "krb", [128, S], BF16)
            vab = _sb(pc, nc, "vab", [128, 32 * 128], BF16)
            qrh = _sb(pc, nc, "qrh", [128, RPC], BF16)
            qhh = _sb(pc, nc, "qhh", [128, RPC], BF16)
            caub = _sb(pc, nc, "caub", [128, NSLOT * 512], BF16)
            bst = _sb(pc, nc, "bst", [128, 64], F32)
            condi = _sb(pc, nc, "condi", [128, 8], I32)
            prefb = _sb(pc, nc, "prefb", [128, 4096], F16)
            ohb = _sb(pc, nc, "ohb", [128, 2 * RPC], BF16)
            rs1 = _sb(pc, nc, "rs1", [1, 128], BF16)
            ct.wait(gp, s_ve)
            ct.wait(gp, s_se)
            ct.wait(gp, s_sc)
            ct.bump(gp.dma_start(out=caub.rearrange("p (s c) -> p s c", s=NSLOT), in_=caus
                                 .rearrange("s p c -> p s c")), s_ld, 16)

            for h in range(NH):
                kv = h // 4
                ct.wait(gp, s_ve)
                ct.wait(gp, s_pe)
                ct.wait(gp, s_se)
                ct.wait(gp, s_sc)
                ct.wait(ve, s_sc)
                if h % 4 == 0:
                    ct.bump(gp.dma_start(out=krb, in_=krD[kv]), s_ld, 16)
                    ct.bump(gp.dma_start(
                        out=vab.rearrange("p (t m) -> p t m", t=32),
                        in_=vD[:, kv * 128:(kv + 1) * 128].rearrange("(t p) m -> p t m", p=128)),
                        s_ld, 16)
                ct.bump(gp.dma_start(out=khb, in_=khD[h]), s_ld, 16)
                ct.bump(gp.dma_start(out=qrh, in_=qrD[h]), s_ld, 16)
                ct.bump(gp.dma_start(out=qhh, in_=qhD[h]), s_ld, 16)
                ct.wait(pe, s_ld)
                for s_ in range(NSLOT):
                    ext = 512 * (s_ + 1)
                    qsl = qhh[:, s_ * 128:(s_ + 1) * 128]
                    for kb in range(s_ + 1):
                        pb = ps[:, (kb % 2) * 512:((kb % 2) + 1) * 512]
                        ct.wait(pe, s_se, ct.val(s_se))
                        ii = pe.matmul(pb, lhsT=qsl, rhs=khb[:, kb * 512:(kb + 1) * 512],
                                       start=True, stop=True)
                        ct.bump(ii, s_pe)
                        ct.wait(se, s_pe)
                        ct.bump(se.activation(
                            out=simb[:, soff[s_] + kb * 512:soff[s_] + (kb + 1) * 512],
                            in_=pb, func=AF.Copy), s_se)
                    ct.wait(ve, s_se)
                    ct.bump(ve.tensor_tensor(
                        out=simb[:, soff[s_] + ext - 512:soff[s_] + ext],
                        in0=simb[:, soff[s_] + ext - 512:soff[s_] + ext],
                        in1=caub[:, s_ * 512:(s_ + 1) * 512], op=A.add), s_ve)
                lo = bst[:, 0:8]
                hi = bst[:, 8:16]
                cnth = bst[:, 16:24]
                mid = bst[:, 24:32]
                cnt = bst[:, 32:40]
                cond = bst[:, 40:48]
                tmp = bst[:, 48:56]
                t2v = bst[:, 56:64]
                ve.memset(lo, -130.0)
                ve.memset(cnth, 0.0)
                ve.drain()
                for stp in range(8):
                    ct.bump(ve.tensor_scalar(out=mid, in0=lo,
                                             scalar1=float(512 >> (stp + 1)),
                                             scalar2=None, op0=A.add), s_ve)
                    ve.drain()
                    dmap = {0: eslb[:, 3584:4096], 1: etb[:, 3072:4096],
                            2: prefb[:, 2560:4096], 3: xbf[:, SIMW + 12288:SIMW + 14336],
                            4: prefb[:, 0:2560], 5: etb[:, 0:3072],
                            6: eslb[:, 0:3584], 7: escb[:, 0:4096]}
                    for s_ in range(NSLOT):
                        ext = 512 * (s_ + 1)
                        ve.tensor_scalar(out=dmap[s_], in0=simb[:, soff[s_]:soff[s_] + ext],
                                         scalar1=mid[:, s_:s_ + 1], scalar2=0.0,
                                         op0=A.is_ge, op1=A.add,
                                         accum_out=cnt[:, s_:s_ + 1])
                    ve.drain()
                    ve.tensor_scalar(out=condi[:, :], in0=cnt, scalar1=float(KEEP),
                                     scalar2=None, op0=A.is_ge)
                    ve.drain()
                    ve.select(out=tmp, mask=condi[:, :], on_true=mid, on_false=lo,
                              add_drain=True)
                    ve.select(out=hi, mask=condi[:, :], on_true=cnth, on_false=cnt,
                              add_drain=True)
                    ve.drain()
                    ve.tensor_copy(out=lo, in_=tmp)
                    ve.tensor_copy(out=cnth, in_=hi)
                    ve.drain()
                ve.tensor_scalar(out=cnth, in0=cnth, scalar1=-1.0, scalar2=float(KEEP),
                                 op0=A.mult, op1=A.add)
                ve.tensor_scalar(out=t2v, in0=lo, scalar1=2.0, scalar2=None, op0=A.add)
                ct.bump(ve.drain(), s_ve)
                ct.wait(pe, s_ve, ct.val(s_ve))
                for s_ in range(NSLOT):
                    ext = 512 * (s_ + 1)
                    sims = simb[:, soff[s_]:soff[s_] + ext]
                    qsl = qrh[:, s_ * 128:(s_ + 1) * 128]
                    for kb in range(s_ + 1):
                        pb = ps[:, (kb % 2) * 512:((kb % 2) + 1) * 512]
                        ct.wait(pe, s_se, ct.val(s_se))
                        ii = pe.matmul(pb, lhsT=qsl, rhs=krb[:, kb * 512:(kb + 1) * 512],
                                       start=True, stop=True)
                        ct.bump(ii, s_pe)
                        ct.wait(se, s_pe)
                        ct.bump(se.activation(out=escb[:, kb * 512:(kb + 1) * 512],
                                              in_=pb, func=AF.Exp, scale=float(1.0 / np.sqrt(HD))),
                                s_se)
                    ct.wait(ve, s_se)
                    ve.tensor_scalar(out=eslb[:, 0:ext], in0=sims, scalar1=lo[:, s_:s_ + 1],
                                     scalar2=None, op0=A.is_equal)
                    ve.drain()
                    ve.tensor_tensor_scan(out=prefb[:, 0:ext], data0=eslb[:, 0:ext],
                                          data1=eslb[:, 0:ext], initial=0.0,
                                          op0=A.add, op1=A.bypass)
                    ve.drain()
                    ve.tensor_scalar(out=prefb[:, 0:ext], in0=prefb[:, 0:ext],
                                     scalar1=cnth[:, s_:s_ + 1], scalar2=None, op0=A.is_le)
                    ve.drain()
                    ve.tensor_tensor(out=eslb[:, 0:ext], in0=eslb[:, 0:ext],
                                     in1=prefb[:, 0:ext], op=A.mult)
                    ve.drain()
                    ve.tensor_scalar(out=etb[:, 0:ext], in0=sims, scalar1=t2v[:, s_:s_ + 1],
                                     scalar2=None, op0=A.is_ge)
                    ve.drain()
                    ve.tensor_tensor(out=eslb[:, 0:ext], in0=eslb[:, 0:ext],
                                     in1=etb[:, 0:ext], op=A.add)
                    ve.drain()
                    ct.bump(ve.tensor_tensor_reduce(out=etb[:, 0:ext], in0=escb[:, 0:ext],
                                                    in1=eslb[:, 0:ext], scale=1.0, scalar=0.0,
                                                    op0=A.mult, op1=A.add,
                                                    accum_out=bst[:, 48 + s_:49 + s_]
                                                    if False else cnt[:, s_:s_ + 1]), s_ve)
                    for kb4 in range((ext + 511) // 512):
                        nblk = min(4, (ext - kb4 * 512) // 128)
                        ct.wait(pe, s_ve, ct.val(s_ve))
                        ct.wait(pe, s_se, ct.val(s_se))
                        for b4 in range(nblk):
                            kb = kb4 * 4 + b4
                            ii = pe.transpose(psT[:, b4 * 128:(b4 + 1) * 128],
                                              in_=etb[:, kb * 128:(kb + 1) * 128], identity=idb)
                        ct.bump(ii, s_pe)
                        ct.wait(se, s_pe)
                        ct.bump(se.activation(out=escb[:, kb4 * 512:kb4 * 512 + nblk * 128],
                                              in_=psT[:, 0:nblk * 128], func=AF.Copy), s_se)
                    ct.wait(pe, s_se)
                    nkb = ext // 128
                    for kb in range(nkb):
                        ii = pe.matmul(ps[:, 1024:1024 + 128], lhsT=vab[:, kb * 128:(kb + 1) * 128],
                                       rhs=escb[:, kb * 128:(kb + 1) * 128],
                                       start=(kb == 0), stop=(kb == nkb - 1))
                    ct.bump(ii, s_pe)
                    ct.wait(ve, s_pe)
                    ve.reciprocal(out=tmp[:, s_:s_ + 1], in_=cnt[:, s_:s_ + 1])
                    ve.drain()
                    ct.bump(ve.tensor_copy(out=sq[:, 0:1], in_=tmp[:, s_:s_ + 1]), s_ve)
                    ct.wait(pe, s_ve)
                    ii = pe.transpose(psT[0:1, 512:640], in_=sq[:, 0:1], identity=idb)
                    ct.bump(ii, s_pe)
                    ct.wait(ve, s_pe)
                    ct.bump(ve.tensor_copy(out=rs1, in_=psT[0:1, 512:640]), s_ve)
                    ct.wait(pe, s_ve)
                    ii = pe.matmul(ps[:, 1536:1536 + 128], lhsT=oner, rhs=rs1, start=True, stop=True)
                    ct.bump(ii, s_pe)
                    ct.wait(ve, s_pe)
                    ve.tensor_copy(out=sq[:, 128:256], in_=ps[:, 1536:1536 + 128])
                    ve.drain()
                    ct.bump(ve.tensor_tensor(out=ohb[:, (h % 2) * RPC + s_ * 128:(h % 2) * RPC + (s_ + 1) * 128],
                                             in0=ps[:, 1024:1024 + 128], in1=sq[:, 128:256],
                                             op=A.mult), s_ve)
                ct.wait(gp, s_ve)
                ct.wait(gp, s_sc)
                ct.bump(gp.dma_start(out=atD[h], in_=ohb[:, (h % 2) * RPC:(h % 2 + 1) * RPC]), s_sc, 16)

        ct.wait(gp, s_sc)
        ct.wait(gp, s_pe)
        ct.wait(gp, s_ve)
        ct.wait(gp, s_se)
        ct.bump(gp.dma_start(out=xbf.rearrange("p (t c) -> p t c", t=NDT),
                             in_=atD.rearrange("t p c -> p t c")), s_ld, 16)
        with ExitStack() as pd:
            xrf = _sb(pd, nc, "xrf", [128, RPC], F32)
            outf = _sb(pd, nc, "outf", [128, RPC], F32)
            at = lambda t: xbf[:, t * RPC:(t + 1) * RPC]

            def wait_x0(ot):
                ct.wait(sy, s_rx)
                ct.wait(sy, s_ve)
                ct.wait(sy, s_sc)
                ct.bump(sy.dma_start(out=xrf, in_=x0T[ot * 128:(ot + 1) * 128, :]), s_rx, 16)

            def loadw_o(ot, buf):
                ct.wait(gp, s_w)
                ct.wait(gp, s_pe)
                ct.bump(gp.dma_start(
                    out=wtl[:, buf * NDT * 128:(buf + 1) * NDT * 128]
                        .rearrange("p (t m) -> p t m", t=NDT),
                    in_=woT[:, ot * 128:(ot + 1) * 128].rearrange("(t p) c -> p t c", p=128)),
                    s_w, 16)

            w_base = ct.val(s_w)
            loadw_o(0, 0)
            ct.wait(pe, s_ld)
            for ot in range(NDT):
                bb = ot % 2
                if ot + 1 < NDT:
                    loadw_o(ot + 1, (ot + 1) % 2)
                ct.wait(pe, s_w, w_base + 16 * (ot + 1))
                wait_x0(ot)
                for rb in range(2):
                    pb = ps[:, rb * 512:(rb + 1) * 512]
                    ct.wait(pe, s_ve, ct.val(s_ve))
                    for t in range(NDT):
                        ii = pe.matmul(pb, lhsT=wtl[:, bb * NDT * 128 + t * 128:bb * NDT * 128 + (t + 1) * 128],
                                       rhs=at(t)[:, rb * 512:(rb + 1) * 512],
                                       start=(t == 0), stop=(t == NDT - 1))
                    ct.bump(ii, s_pe)
                    ct.wait(ve, s_pe)
                    ct.wait(ve, s_rx)
                    ct.wait(ve, s_st)
                    ct.bump(ve.tensor_tensor(out=outf[:, rb * 512:(rb + 1) * 512], in0=pb,
                                             in1=xrf[:, rb * 512:(rb + 1) * 512], op=A.add), s_ve)
                ct.wait(sy, s_ve)
                ct.wait(sy, s_st)
                ct.bump(sy.dma_start(out=x1T[ot * 128:(ot + 1) * 128, :], in_=outf), s_st, 16)
        ct.wait(gp, s_st)
    mybir.codegen_inst_isa_subclasses(nc)
    return nc




def _rope_tables():
    inv = 1.0 / (10000.0 ** (np.arange(0, HD, 2, dtype=np.float64) / HD))
    t = np.arange(S, dtype=np.float64)
    fr = np.outer(t, inv)
    emb = np.concatenate([fr, fr], -1)
    return (np.cos(emb).T.astype(np.float32).copy(),
            np.sin(emb).T.astype(np.float32).copy())


def _rotT():
    R = np.zeros((HD, HD), np.float32)
    for d in range(64):
        R[d, d + 64] = -1.0
        R[d + 64, d] = 1.0
    return np.ascontiguousarray(R.T)


def _causal_tiles(c):
    j = c % 4
    out = np.zeros((NSLOT, 128, 512), np.float32)
    for s in range(NSLOT):
        q0 = 128 * (4 * s + j)
        kk = 512 * s + np.arange(512)[None, :]
        qq = q0 + np.arange(128)[:, None]
        out[s] = np.where(kk > qq, NEGC, 0.0)
    return out


_L0ATTN_CACHE = {}


def run_l0attn(x0_full, Wq, Wk, Wv, Wo, p1, b1h, p2, b2h):
    if "nc" not in _L0ATTN_CACHE:
        _L0ATTN_CACHE["nc"] = build_l0attn()
    nc = _L0ATTN_CACHE["nc"]
    cosT, sinT = _rope_tables()
    common = dict(
        wqT=np.ascontiguousarray(Wq.T), wkT=np.ascontiguousarray(Wk.T),
        wvT=np.ascontiguousarray(Wv.T), woT=np.ascontiguousarray(Wo.T),
        p1w=p1, p2w=p2,
        hb1=np.ascontiguousarray(b1h.reshape(NH, HD).T),
        hb2=np.ascontiguousarray(b2h.reshape(NH, HD).T),
        cosB=cosT, sinB=sinT, rotT=_rotT(),
    )
    xsh = _shard_xT(x0_full)
    in_maps = []
    for c in range(NCORES):
        rows = core_rows(c)
        in_maps.append(dict(
            x0T=xsh[c],
            x0TB=np.ascontiguousarray(x0_full[c // 4].T),
            cosO=np.ascontiguousarray(cosT[:, rows]),
            sinO=np.ascontiguousarray(sinT[:, rows]),
            caus=_causal_tiles(c), **common))
    res = run_bass_kernel_spmd(nc, in_maps, core_ids=list(range(NCORES)))
    return _gather_xT([r["x1T"] for r in res.results])


_MISC_CACHE = {}


def kernel(**inputs):
    x0 = np.asarray(inputs["L0_attn_hidden_states"], np.float32)
    x1 = run_l0attn(x0,
                    inputs["L0_attn_Wq"], inputs["L0_attn_Wk"],
                    inputs["L0_attn_Wv"], inputs["L0_attn_Wo"],
                    inputs["L0_attn_hash_proj1"], inputs["L0_attn_hash_bias1"],
                    inputs["L0_attn_hash_proj2"], inputs["L0_attn_hash_bias2"])
    if "labT" not in _MISC_CACHE:
        _MISC_CACHE["labT"] = _dropout_labels()
    x2 = run_l0ffn(x1, inputs["L0_ffn_W0"], inputs["L0_ffn_b0"],
                   inputs["L0_ffn_A"], inputs["L0_ffn_B"], _MISC_CACHE["labT"])
    x3 = run_l1attn(x2, inputs["L1_attn_ln1_w"], inputs["L1_attn_ln2_w"],
                    (inputs["L1_attn_qkv_qw"], inputs["L1_attn_qkv_qz"],
                     inputs["L1_attn_qkv_sc"]),
                    (inputs["L1_attn_o_qw"], inputs["L1_attn_o_qz"],
                     inputs["L1_attn_o_sc"]),
                    (inputs["L1_attn_gu_qw"], inputs["L1_attn_gu_qz"],
                     inputs["L1_attn_gu_sc"]),
                    (inputs["L1_attn_dn_qw"], inputs["L1_attn_dn_qz"],
                     inputs["L1_attn_dn_sc"]))
    out = run_l1ffn(x3, inputs["L1_ffn_gate_w"], inputs["L1_ffn_gate_b"],
                    inputs["L1_ffn_w1"], inputs["L1_ffn_b1"],
                    inputs["L1_ffn_w2"], inputs["L1_ffn_b2"])
    return np.ascontiguousarray(out, dtype=np.float32)
